# version 16
# baseline (speedup 1.0000x reference)
import numpy as np
import ml_dtypes
from contextlib import ExitStack

import concourse.bass as bass
import concourse.mybir as mybir
from concourse.bass_utils import run_bass_kernel_spmd

F32 = mybir.dt.float32
BF16 = mybir.dt.bfloat16
AF = mybir.ActivationFunctionType
ALU = mybir.AluOpType
BF = ml_dtypes.bfloat16

T = 2048
HID = 2048
QKV = 8192
VD = 4096
NK, DK = 16, 128
NV, DV = 32, 128
P2C = 128
NCH = T // P2C
NCC = HID // 128
NFQ = QKV // 128
NFZ = VD // 128
CL = 60.0
EPS_RMS = 1e-6
LN_EPS = 1e-5

ENGINES = ("tensor", "vector", "scalar", "gpsimd", "sync")


class Prog:

    def __init__(self):
        self.streams = {e: [] for e in ENGINES}
        self.cnt = {}
        self.waited = {e: {} for e in ENGINES}
        self.sem = {}
        self.last = {e: None for e in ENGINES}
        self.dma_sems = set()

    def new_sems(self, *names):
        for n in names:
            self.cnt[n] = 0

    def bump(self, name, k=1):
        self.cnt[name] = self.cnt.get(name, 0) + k
        return (name, self.cnt[name])

    def full(self, name):
        return (name, self.cnt.get(name, 0))

    def step(self, eng, waits=(), inc=None, k=1):
        waits = list(waits)
        if eng in ("vector", "scalar", "gpsimd") and self.last[eng] is not None:
            waits.append(self.last[eng])
        tok = self.bump(inc, k) if inc else None
        if inc is not None:
            self.last[eng] = tok
            if k == 16:
                self.dma_sems.add(inc)
        wl = []
        for w in waits:
            if w is None:
                continue
            s, v = w
            if v > self.waited[eng].get(s, 0):
                self.waited[eng][s] = v
                wl.append((s, v))

        def reg(fn):
            def emit(e):
                for (s, v) in wl:
                    e.wait_ge(self.sem[s], v)
                inst = fn(e)
                if inc is not None:
                    assert inst is not None, "step with inc must return last inst"
                    inst.then_inc(self.sem[inc], k)
            self.streams[eng].append(emit)
            return tok
        return reg


def ceil_div(a, b):
    return (a + b - 1) // b


def build(phases=(1, 2, 25, 34), dump=False):
    nc = bass.Bass("TRN2", target_bir_lowering=False)
    p = Prog()
    sems_needed = []
    for ph in ("0", "1", "2", "3", "4"):
        for s in ("pe", "dv", "ac", "po", "qs", "qg"):
            sems_needed.append(s + ph)
    sems_needed.append("st")
    p.new_sems(*sems_needed)

    def din(name, shape, dtype=F32):
        return nc.dram_tensor(name, shape, dtype, kind="ExternalInput").ap()

    xT = din("xT", [HID, T])
    wqkvT = din("wqkvT", [HID, QKV])
    wzT = din("wzT", [HID, VD])
    wbaT = din("wbaT", [HID, 64])
    woutT = din("woutT", [VD, HID])
    upT = din("upT", [HID, QKV])
    downT = din("downT", [QKV, HID])
    convw = din("convw", [QKV, 4])
    dtb_c = din("dtb_c", [32, 1])
    alog_c = din("alog_c", [32, 1])
    normw_c = din("normw_c", [128, 1])
    upsc_c = din("upsc_c", [128, 64])
    dssc_c = din("dssc_c", [128, 16])
    osc_c = din("osc_c", [1, 1])
    identb_d = din("identb_d", [128, 128], BF16)
    identf_d = din("identf_d", [128, 128], F32)
    maskL_d = din("maskL_d", [128, 128], BF16)
    maskUs_d = din("maskUs_d", [128, 128], BF16)
    maskUi_d = din("maskUi_d", [128, 128], BF16)
    Lz_d = din("Lz_d", [6, 128], BF16)

    outT = nc.dram_tensor("outT", [HID, T], F32, kind="ExternalOutput").ap()

    skind = "ExternalOutput" if dump else "Internal"
    qkvc_d = nc.dram_tensor("qkvc_s", [QKV, T], BF16, kind=skind).ap()
    zsil_d = nc.dram_tensor("zsil_s", [VD, T], BF16, kind=skind).ap()
    og_d = nc.dram_tensor("og_s", [VD, T], BF16, kind=skind).ap()
    x1f_d = nc.dram_tensor("x1f_s", [HID, T], F32, kind=skind).ap()
    dbg_d = nc.dram_tensor("dbg_s", [64, T], F32, kind=skind).ap() if dump else None
    dbg2_d = nc.dram_tensor("dbg2_s", [HID, 512], F32, kind=skind).ap() if dump else None
    dbg4_d = nc.dram_tensor("dbg4_s", [64, T], F32, kind=skind).ap() if dump else None
    dbg5_d = nc.dram_tensor("dbg5_s", [64, T], F32, kind=skind).ap() if dump else None
    dbg3_d = nc.dram_tensor("dbg3_s", [1024, 512], BF16, kind=skind).ap() if dump else None

    ctx = ExitStack()
    with ctx:
        e_ = ctx.enter_context

        identb = e_(nc.sbuf_tensor("identb", [128, 128], BF16))
        identf = e_(nc.sbuf_tensor("identf", [128, 128], F32))
        maskL = e_(nc.sbuf_tensor("maskL", [128, 128], BF16))
        maskUs = e_(nc.sbuf_tensor("maskUs", [128, 128], BF16))
        maskUi = e_(nc.sbuf_tensor("maskUi", [128, 128], BF16))
        onesb = e_(nc.sbuf_tensor("onesb", [1, 128], BF16))
        onescb = e_(nc.sbuf_tensor("onescb", [128, 1], BF16))
        normw = e_(nc.sbuf_tensor("normw", [128, 1], F32))
        dtb = e_(nc.sbuf_tensor("dtb", [32, 1], F32))
        negexpA = e_(nc.sbuf_tensor("negexpA", [32, 1], F32))
        upsc = e_(nc.sbuf_tensor("upsc", [128, 64], F32))
        dssc = e_(nc.sbuf_tensor("dssc", [128, 16], F32))
        dseff = e_(nc.sbuf_tensor("dseff", [128, 16], F32))
        oscal = e_(nc.sbuf_tensor("oscal", [1, 1], F32))
        onesf = e_(nc.sbuf_tensor("onesf", [1, 128], F32))
        beta_b = e_(nc.sbuf_tensor("beta_b", [32, T], BF16))
        lnbeta = e_(nc.sbuf_tensor("lnbeta", [32, T], F32))
        cpad = e_(nc.sbuf_tensor("cpad", [32, T + 128], F32))

        ld0 = []
        for (dst, src) in [(identb, identb_d), (identf, identf_d), (maskL, maskL_d),
                           (maskUs, maskUs_d), (maskUi, maskUi_d), (normw, normw_c),
                           (dtb, dtb_c), (upsc, upsc_c), (dssc, dssc_c), (osc := oscal, osc_c),
                           (negexpA, alog_c)]:
            @p.step("sync", inc="c0", k=16)
            def _(e, dst=dst, src=src):
                return e.dma_start(out=dst[:, :], in_=src)
        ld0_tok = p.full("c0")
        alog_tok = ld0_tok

        @p.step("vector", waits=[ld0_tok], inc="dv0")
        def _(e):
            return e.memset(onesb[:, :], 1.0)

        @p.step("vector", inc="dv0")
        def _(e):
            return e.memset(onescb[:, :], 1.0)

        @p.step("vector", inc="dv0")
        def _(e):
            return e.memset(cpad[:, 0:128], 0.0)

        @p.step("vector", inc="dv0")
        def _(e):
            return e.memset(onesf[:, :], 1.0)
        memset_tok = ("dv0", p.cnt["dv0"])

        @p.step("scalar", waits=[alog_tok], inc="ac0")
        def _(e):
            return e.activation(out=negexpA[:, :], in_=negexpA[:, :], func=AF.Exp)
        @p.step("vector", waits=[("ac0", p.cnt["ac0"])], inc="dv0")
        def _(e):
            return e.tensor_scalar(out=negexpA[:, :], in0=negexpA[:, :], scalar1=-1.0,
                                   scalar2=None, op0=ALU.mult)
        negexpA_tok = ("dv0", p.cnt["dv0"])

        env = dict(locals())
        if 1 in phases:
            emit_p1(nc, p, ctx, env)
        if 2 in phases:
            emit_p2(nc, p, ctx, env)
        if 25 in phases:
            emit_p25(nc, p, ctx, env)
        if 34 in phases:
            emit_p34(nc, p, ctx, env)

        @p.step("sync", waits=[("st", p.cnt["st"])] +
                [(s, p.cnt[s]) for s in sorted(p.dma_sems)])
        def _(e):
            return None

        with ExitStack() as sctx:
            for name in p.cnt:
                p.sem[name] = sctx.enter_context(nc.semaphore(name))
            blk = sctx.enter_context(nc.Block())

            @blk.tensor
            def _(eng):
                for f in p.streams["tensor"]:
                    f(eng)

            @blk.vector
            def _(eng):
                for f in p.streams["vector"]:
                    f(eng)

            @blk.scalar
            def _(eng):
                for f in p.streams["scalar"]:
                    f(eng)

            @blk.gpsimd
            def _(eng):
                for f in p.streams["gpsimd"]:
                    f(eng)

            @blk.sync
            def _(eng):
                for f in p.streams["sync"]:
                    f(eng)

    return nc


def emit_p1(nc, p, ctx, env):
    e_ = ctx.enter_context
    g = env
    xT_d, wqkvT, wzT, wbaT, convw = g["xT"], g["wqkvT"], g["wzT"], g["wbaT"], g["convw"]
    qkvc_d, zsil_d = g["qkvc_d"], g["zsil_d"]
    beta_b, lnbeta, cpad = g["beta_b"], g["lnbeta"], g["cpad"]
    dtb, negexpA = g["dtb"], g["negexpA"]
    memset_tok = g["memset_tok"]
    negexpA_tok = g["negexpA_tok"]

    p1 = ExitStack()
    with p1:
        ee = p1.enter_context
        xTb = ee(nc.sbuf_tensor("xTb", [128, NCC * T], BF16))
        wbuf = [ee(nc.sbuf_tensor(f"wbuf{i}", [128, NCC * 128], BF16)) for i in range(2)]
        wf32 = [ee(nc.sbuf_tensor(f"wf32{i}", [128, NCC * 128], F32)) for i in range(2)]
        qrow = [ee(nc.sbuf_tensor(f"qrow{i}", [128, 3 + T], F32)) for i in range(2)]
        acc1 = ee(nc.sbuf_tensor("acc1", [128, T], F32))
        acc2 = ee(nc.sbuf_tensor("acc2", [128, T], F32))
        acc4 = ee(nc.sbuf_tensor("acc4", [128, T], F32))
        acc5 = ee(nc.sbuf_tensor("acc5", [128, T], F32))
        obuf = [ee(nc.sbuf_tensor(f"obuf{i}", [128, T], BF16)) for i in range(2)]
        cw = ee(nc.sbuf_tensor("cw", [128, NFQ * 4], F32))
        wba = ee(nc.sbuf_tensor("wba", [128, NCC * 64], BF16))
        bdtraw = ee(nc.sbuf_tensor("bdtraw", [64, T], F32))
        t32a = ee(nc.sbuf_tensor("t32a", [32, T], F32))
        t32b = ee(nc.sbuf_tensor("t32b", [32, T], F32))
        zeros32 = ee(nc.sbuf_tensor("zeros32", [32, T], F32))
        pq = [ee(nc.psum_tensor(f"pq{i}", [128, 512], F32)) for i in range(4)]
        pba = ee(nc.psum_tensor("pba", [64, 512], F32))

        for cc in range(NCC):
            @p.step("gpsimd", inc="d_x", k=16)
            def _(e, cc=cc):
                return e.dma_start(out=xTb[:, cc * T:(cc + 1) * T],
                                   in_=xT_d[cc * 128:(cc + 1) * 128, :])
        x_tok = p.full("d_x")

        @p.step("vector", inc="dv1")
        def _(e):
            return e.memset(qrow[0][:, 0:3], 0.0)

        @p.step("vector", inc="dv1")
        def _(e):
            return e.memset(qrow[1][:, 0:3], 0.0)
        qpad_tok = ("dv1", p.cnt["dv1"])

        @p.step("sync", inc="d_cw", k=16)
        def _(e):
            return e.dma_start(
                out=cw[:, :].rearrange("p (fc j) -> p fc j", j=4),
                in_=convw.rearrange("(fc p) j -> p fc j", p=128))
        cw_tok = p.full("d_cw")

        @p.step("gpsimd", inc="d_wba", k=16)
        def _(e):
            return e.dma_start(
                out=wba[:, :].rearrange("p (cc f) -> p cc f", f=64),
                in_=wbaT.rearrange("(cc p) f -> p cc f", p=128))
        wba_tok = p.full("d_wba")

        sched = [("qkv", fc) for fc in range(NFQ)] + [("z", fc) for fc in range(NFZ)]

        wload_tok = [None, None]
        wfree_tok = [None, None]
        qrow_free = [None, None]
        obuf_free = [None, None]
        accA_free = None
        acc3_free = None
        zfree = [None, None]
        acc2_free = None
        pq_free = [None, None, None, None]

        wcast_tok = [None, None]

        def wload(i, kind, fc):
            src = wqkvT if kind == "qkv" else wzT
            @p.step("sync", waits=[wcast_tok[i]], inc=f"d_w{i}", k=16)
            def _(e, src=src, fc=fc, i=i):
                return e.dma_start(
                    out=wf32[i][:, :].rearrange("p (cc f) -> p cc f", f=128),
                    in_=src[:, fc * 128:(fc + 1) * 128].rearrange("(cc p) f -> p cc f", p=128))
            wl = p.full(f"d_w{i}")

            @p.step("vector", waits=[wl, wfree_tok[i]], inc="dv1")
            def _(e, i=i):
                return e.tensor_copy(out=wbuf[i][:, :], in_=wf32[i][:, :])
            wload_tok[i] = ("dv1", p.cnt["dv1"])
            wcast_tok[i] = wload_tok[i]

        wload(0, *sched[0])
        wload(1, *sched[1])

        for si, (kind, fc) in enumerate(sched):
            par = si % 2
            @p.step("tensor", waits=[wload_tok[par], x_tok, pq_free[0], pq_free[1],
                                     pq_free[2], pq_free[3]], inc="pe1")
            def _(e, par=par):
                inst = None
                for cc in range(NCC):
                    for ts in range(4):
                        inst = e.matmul(pq[ts][:, :],
                                        wbuf[par][:, cc * 128:(cc + 1) * 128],
                                        xTb[:, cc * T + ts * 512: cc * T + (ts + 1) * 512],
                                        start=(cc == 0), stop=(cc == NCC - 1))
                return inst
            mm_tok = ("pe1", p.cnt["pe1"])
            mm_toks = [mm_tok] * 4

            evac_toks = []
            for ts in range(4):
                @p.step("scalar", waits=[mm_tok, qrow_free[par], qpad_tok], inc="ac1")
                def _(e, par=par, ts=ts):
                    return e.copy(out=qrow[par][:, 3 + ts * 512: 3 + (ts + 1) * 512],
                                  in_=pq[ts][:, :])
                evac_toks.append(("ac1", p.cnt["ac1"]))
                pq_free[ts] = evac_toks[-1]
            qrow_tok = evac_toks[-1]

            if kind == "qkv":
                @p.step("vector", waits=[qrow_tok, cw_tok, accA_free], inc="dv1")
                def _(e, par=par, fc=fc):
                    return e.tensor_scalar(out=acc1[:, :], in0=qrow[par][:, 3:3 + T],
                                           scalar1=cw[:, fc * 4 + 3: fc * 4 + 4],
                                           scalar2=None, op0=ALU.mult)

                @p.step("vector", inc="dv1")
                def _(e, par=par, fc=fc):
                    return e.scalar_tensor_tensor(out=acc2[:, :], in0=qrow[par][:, 2:2 + T],
                                                  scalar=cw[:, fc * 4 + 2: fc * 4 + 3],
                                                  in1=acc1[:, :], op0=ALU.mult, op1=ALU.add)

                @p.step("vector", inc="dv1")
                def _(e, par=par, fc=fc):
                    return e.scalar_tensor_tensor(out=acc1[:, :], in0=qrow[par][:, 1:1 + T],
                                                  scalar=cw[:, fc * 4 + 1: fc * 4 + 2],
                                                  in1=acc2[:, :], op0=ALU.mult, op1=ALU.add)
                @p.step("vector", waits=[obuf_free[par]], inc="dv1")
                def _(e, par=par, fc=fc):
                    return e.scalar_tensor_tensor(out=obuf[par][:, :],
                                                  in0=qrow[par][:, 0:T],
                                                  scalar=cw[:, fc * 4: fc * 4 + 1],
                                                  in1=acc1[:, :],
                                                  op0=ALU.mult, op1=ALU.add)
                pool_tok = ("dv1", p.cnt["dv1"])
                qrow_free[par] = pool_tok
                accA_free = pool_tok

                @p.step("sync", waits=[pool_tok], inc=f"d_ob{par}", k=16)
                def _(e, par=par, fc=fc):
                    return e.dma_start(out=qkvc_d[fc * 128:(fc + 1) * 128, :],
                                       in_=obuf[par][:, :])
                obuf_free[par] = p.full(f"d_ob{par}")
            else:
                if zfree[0] is None and zfree[1] is None:
                    zfree = [accA_free, accA_free]
                za = [acc1, acc4][par]
                zb = [acc2, acc5][par]
                @p.step("scalar", waits=[qrow_tok, zfree[par]], inc="ac1")
                def _(e, par=par, za=za):
                    return e.activation(out=za[:, :], in_=qrow[par][:, 3:3 + T],
                                        func=AF.Exp, scale=-1.0)
                exp_tok = ("ac1", p.cnt["ac1"])

                @p.step("vector", waits=[exp_tok], inc="dv1")
                def _(e, za=za, zb=zb):
                    return e.tensor_scalar(out=zb[:, :], in0=za[:, :], scalar1=1.0,
                                           scalar2=None, op0=ALU.add)

                @p.step("vector", inc="dv1")
                def _(e, za=za, zb=zb):
                    return e.reciprocal(out=za[:, :], in_=zb[:, :])

                @p.step("vector", waits=[obuf_free[par]], inc="dv1")
                def _(e, par=par, za=za):
                    return e.tensor_mul(out=obuf[par][:, :], in0=qrow[par][:, 3:3 + T],
                                        in1=za[:, :])
                sil_tok = ("dv1", p.cnt["dv1"])
                qrow_free[par] = sil_tok
                zfree[par] = sil_tok
                accA_free = sil_tok if par == 0 else accA_free

                @p.step("sync", waits=[sil_tok], inc=f"d_ob{par}", k=16)
                def _(e, par=par, fc=fc):
                    return e.dma_start(out=zsil_d[fc * 128:(fc + 1) * 128, :],
                                       in_=obuf[par][:, :])
                obuf_free[par] = p.full(f"d_ob{par}")

            wfree_tok[par] = mm_toks[-1]
            if si + 2 < len(sched):
                wload(par, *sched[si + 2])

        ba_mm = []
        pba_free = None
        for ts in range(4):
            @p.step("tensor", waits=[wba_tok, x_tok, pba_free], inc="pe1")
            def _(e, ts=ts):
                inst = None
                for cc in range(NCC):
                    inst = e.matmul(pba[:, :],
                                    wba[:, cc * 64:(cc + 1) * 64],
                                    xTb[:, cc * T + ts * 512: cc * T + (ts + 1) * 512],
                                    start=(cc == 0), stop=(cc == NCC - 1))
                return inst
            mm_tok = ("pe1", p.cnt["pe1"])

            @p.step("scalar", waits=[mm_tok], inc="ac1")
            def _(e, ts=ts):
                return e.copy(out=bdtraw[:, ts * 512:(ts + 1) * 512], in_=pba[:, :])
            ba_mm.append(("ac1", p.cnt["ac1"]))
            pba_free = ba_mm[-1]
        braw_tok = ba_mm[-1]

        @p.step("vector", waits=[braw_tok, memset_tok], inc="dv1")
        def _(e):
            return e.tensor_scalar(out=t32a[:, :], in0=bdtraw[32:64, :],
                                   scalar1=dtb[:, :], scalar2=None, op0=ALU.add)
        xp_tok = ("dv1", p.cnt["dv1"])

        @p.step("scalar", waits=[xp_tok], inc="ac1")
        def _(e):
            return e.activation(out=t32b[:, :], in_=t32a[:, :], func=AF.Abs)

        @p.step("scalar", inc="ac1")
        def _(e):
            return e.activation(out=t32b[:, :], in_=t32b[:, :], func=AF.Exp, scale=-1.0)

        @p.step("scalar", inc="ac1")
        def _(e):
            return e.activation(out=t32b[:, :], in_=t32b[:, :], func=AF.Ln, bias=1.0)
        l_tok = ("ac1", p.cnt["ac1"])

        @p.step("vector", waits=[l_tok, negexpA_tok], inc="dv1")
        def _(e):
            return e.tensor_scalar(out=t32a[:, :], in0=t32a[:, :], scalar1=0.0,
                                   scalar2=None, op0=ALU.max)

        @p.step("vector", inc="dv1")
        def _(e):
            return e.tensor_add(out=t32a[:, :], in0=t32a[:, :], in1=t32b[:, :])

        @p.step("vector", inc="dv1")
        def _(e):
            return e.tensor_scalar(out=t32a[:, :], in0=t32a[:, :],
                                   scalar1=negexpA[:, :], scalar2=None, op0=ALU.mult)

        @p.step("vector", inc="dv1")
        def _(e):
            return e.memset(zeros32[:, :], 0.0)
        gval_tok = ("dv1", p.cnt["dv1"])

        if g["dbg_d"] is not None:
            @p.step("sync", waits=[gval_tok], inc="st", k=16)
            def _(e):
                return e.dma_start(out=g["dbg5_d"][0:32, :], in_=t32a[:, :])

            @p.step("sync", inc="st", k=16)
            def _(e):
                return e.dma_start(out=g["dbg5_d"][32:64, :], in_=t32b[:, :])
        dump5 = p.full("st")

        @p.step("vector", waits=[dump5], inc="dv1")
        def _(e):
            return e.tensor_tensor_scan(out=cpad[:, 128:], data0=t32a[:, :],
                                        data1=zeros32[:, :], initial=0.0,
                                        op0=ALU.add, op1=ALU.add)
        g["cpad_tok"] = ("dv1", p.cnt["dv1"])

        @p.step("scalar", waits=[braw_tok, g["cpad_tok"]], inc="ac1")
        def _(e):
            return e.activation(out=t32b[:, :], in_=bdtraw[0:32, :], func=AF.Abs)

        @p.step("scalar", inc="ac1")
        def _(e):
            return e.activation(out=t32b[:, :], in_=t32b[:, :], func=AF.Exp, scale=-1.0)

        @p.step("scalar", inc="ac1")
        def _(e):
            return e.activation(out=t32b[:, :], in_=t32b[:, :], func=AF.Ln, bias=1.0)
        lb_tok = ("ac1", p.cnt["ac1"])

        @p.step("vector", waits=[lb_tok], inc="dv1")
        def _(e):
            return e.tensor_scalar(out=lnbeta[:, :], in0=bdtraw[0:32, :], scalar1=0.0,
                                   scalar2=None, op0=ALU.min)

        @p.step("vector", inc="dv1")
        def _(e):
            return e.tensor_sub(out=lnbeta[:, :], in0=lnbeta[:, :], in1=t32b[:, :])
        g["lnbeta_tok"] = ("dv1", p.cnt["dv1"])

        @p.step("scalar", waits=[braw_tok], inc="ac1")
        def _(e):
            return e.activation(out=t32a[:, :], in_=bdtraw[0:32, :], func=AF.Exp,
                                scale=-1.0)
        e2_tok = ("ac1", p.cnt["ac1"])
        @p.step("vector", waits=[e2_tok], inc="dv1")
        def _(e):
            return e.tensor_scalar(out=t32a[:, :], in0=t32a[:, :], scalar1=1.0,
                                   scalar2=None, op0=ALU.add)

        @p.step("vector", inc="dv1")
        def _(e):
            return e.reciprocal(out=t32b[:, :], in_=t32a[:, :])

        @p.step("vector", inc="dv1")
        def _(e):
            return e.tensor_copy(out=beta_b[:, :], in_=t32b[:, :])
        g["beta_tok"] = ("dv1", p.cnt["dv1"])

        if g["dbg_d"] is not None:
            @p.step("sync", waits=[g["beta_tok"], g["cpad_tok"], g["lnbeta_tok"]],
                    inc="st", k=16)
            def _(e):
                return e.dma_start(out=g["dbg_d"][0:32, :], in_=cpad[:, 128:])

            @p.step("sync", inc="st", k=16)
            def _(e):
                return e.dma_start(out=g["dbg_d"][32:64, :], in_=lnbeta[:, :])

            @p.step("sync", inc="st", k=16)
            def _(e):
                return e.dma_start(out=g["dbg4_d"], in_=bdtraw[:, :])


def emit_p2(nc, p, ctx, env):
    g = env
    qkvc_d, zsil_d, og_d = g["qkvc_d"], g["zsil_d"], g["og_d"]
    beta_b, lnbeta, cpad = g["beta_b"], g["lnbeta"], g["cpad"]
    identb, identf = g["identb"], g["identf"]
    maskL, maskUs, maskUi = g["maskL"], g["maskUs"], g["maskUi"]
    onesb, onescb, normw = g["onesb"], g["onescb"], g["normw"]
    onesf = g["onesf"]
    Lz_d = g["Lz_d"]

    bar = [("pe1", p.cnt["pe1"]), ("dv1", p.cnt["dv1"]), ("ac1", p.cnt["ac1"]),
           ("po1", p.cnt["po1"]), p.full("d_ob0"), p.full("d_ob1"), p.full("st")]

    p2 = ExitStack()
    with p2:
        ee = p2.enter_context
        qf = [ee(nc.sbuf_tensor(f"qf{i}", [128, NK * 128], BF16)) for i in range(2)]
        kf1 = ee(nc.sbuf_tensor("kf1", [128, NK * 128], BF16))
        kf = [kf1, kf1]
        ktm2 = [ee(nc.sbuf_tensor(f"ktm{i}", [128, NK * 128], BF16)) for i in range(2)]
        qtm = ee(nc.sbuf_tensor("qtm", [128, NK * 128], BF16))
        vtm = ee(nc.sbuf_tensor("vtm", [128, NV * 128], BF16))
        zf = ee(nc.sbuf_tensor("zf", [128, NV * 128], BF16))
        Sb = ee(nc.sbuf_tensor("Sb", [128, NK * 256], BF16))
        Gl = ee(nc.sbuf_tensor("Gl", [128, NK * 128], BF16))
        Gu = ee(nc.sbuf_tensor("Gu", [128, NK * 128], BF16))
        Qu = ee(nc.sbuf_tensor("Qu", [128, NK * 128], BF16))
        Aw = ee(nc.sbuf_tensor("Aw", [128, NV * 128], BF16))
        ATw = ee(nc.sbuf_tensor("ATw", [128, NV * 128], BF16))
        Wow = ee(nc.sbuf_tensor("Wow", [128, NV * 128], BF16))
        R3w = ee(nc.sbuf_tensor("R3w", [128, NV * 128], BF16))
        LU1 = ee(nc.sbuf_tensor("LU1", [6, NV * 128], BF16))
        LU2 = LU1
        RU1 = ee(nc.sbuf_tensor("RU1", [6, NV * 128], BF16))
        RU2 = ee(nc.sbuf_tensor("RU2", [6, NV * 128], BF16))
        RU3 = ee(nc.sbuf_tensor("RU3", [6, NV * 128], BF16))
        Lz = ee(nc.sbuf_tensor("Lz", [6, 128], BF16))
        Aw2 = ee(nc.sbuf_tensor("Aw2", [128, NV * 128], BF16))
        ATw2 = ee(nc.sbuf_tensor("ATw2", [128, NV * 128], BF16))
        xkbw = ee(nc.sbuf_tensor("xkbw", [128, NV * 128], BF16))
        ubv = ee(nc.sbuf_tensor("ubv", [128, 512], F32))
        Lqsc = [ee(nc.sbuf_tensor(f"Lqsc{L}", [128, 512], BF16)) for L in range(2)]
        Lobf = [ee(nc.sbuf_tensor(f"Lobf{L}", [128, 512], BF16)) for L in range(2)]
        Lkde = [ee(nc.sbuf_tensor(f"Lkde{L}", [128, 512], BF16)) for L in range(2)]
        Lrof = [ee(nc.sbuf_tensor(f"Lrof{L}", [1, 512], F32)) for L in range(2)]
        Lrob = [ee(nc.sbuf_tensor(f"Lrob{L}", [1, 512], BF16)) for L in range(2)]
        Usp = Lqsc[0]
        Utr = xkbw

        Ur1 = ee(nc.sbuf_tensor("Ur1", [128, 32], F32))
        Ur2 = ee(nc.sbuf_tensor("Ur2", [128, 32], F32))
        Etmp2 = [ee(nc.sbuf_tensor(f"Etmp{i}", [128, 512], F32)) for i in range(2)]
        trash = ee(nc.sbuf_tensor("trash", [128, 128], BF16))
        cl60 = ee(nc.sbuf_tensor("cl60", [128, 1], F32))
        ogb1 = ee(nc.sbuf_tensor("ogb1", [128, NV * 128], BF16))
        ogb = [ogb1, ogb1]
        sm = {}
        for n in ("cTt", "cpTt", "lnbT", "bTf", "U1", "U2n", "U3", "U3s",
                  "negeU1s", "Edec", "Sdec", "Cp0rep", "CCrep", "targ"):
            sm[n] = ee(nc.sbuf_tensor("sm_" + n, [128, 32], F32))
        bTb = ee(nc.sbuf_tensor("bTb", [128, 32], BF16))
        nk2 = ee(nc.sbuf_tensor("nk2", [128, 16], F32))
        nq2 = ee(nc.sbuf_tensor("nq2", [128, 16], F32))
        lnk = ee(nc.sbuf_tensor("lnk", [128, 16], F32))
        lnq = ee(nc.sbuf_tensor("lnq", [128, 16], F32))
        rA = ee(nc.sbuf_tensor("rA", [1, 64], F32))
        ptr_b = ee(nc.psum_tensor("ptr_b", [128, 1024], BF16))
        ptr_f = ee(nc.psum_tensor("ptr_f", [128, 512], F32))
        pe_a = ee(nc.psum_tensor("pe_a", [128, 512], F32))
        pe_b = ee(nc.psum_tensor("pe_b", [128, 512], F32))
        ps_a = ee(nc.psum_tensor("ps_a", [128, 512], F32))
        ps_b = ee(nc.psum_tensor("ps_b", [128, 512], F32))
        px = ee(nc.psum_tensor("px", [128, 512], F32))
        py = ee(nc.psum_tensor("py", [128, 512], F32))

        @p.step("sync", waits=bar, inc="d_lz", k=16)
        def _(e):
            return e.dma_start(out=Lz[:, :], in_=Lz_d)
        lz_tok = p.full("d_lz")

        init_toks = []
        first = True
        for stg in (LU1, RU1, RU2, RU3):
            @p.step("vector", waits=bar if first else [], inc="dv2")
            def _(e, stg=stg):
                return e.memset(stg[:, :], 1.0)
            first = False
        @p.step("vector", inc="dv2")
        def _(e):
            return e.memset(Sb[:, :], 0.0)

        @p.step("vector", inc="dv2")
        def _(e):
            inst = None
            for ci in range(4):
                inst = e.memset(Usp[:, ci * 128 + 96:(ci + 1) * 128], 0.0)
            return inst

        @p.step("vector", inc="dv2")
        def _(e):
            return e.memset(cl60[:, :], CL)
        init_tok = ("dv2", p.cnt["dv2"])

        tok = {
            "qf": [None, None], "kf": [None, None],
            "ktm": None, "qtm": None, "vtm": None, "zf": None,
            "stg": init_tok,
            "ogb": [None, None],
            "Sb_w": init_tok,
        }
        ld = {}

        def chunk_loads(ch, firstwaits=()):
            par = ch % 2
            t0 = ch * P2C
            @p.step("gpsimd", waits=list(firstwaits) + [tok["qf"][par]],
                    inc=f"d_qf{par}", k=16)
            def _(e):
                return e.dma_start(
                    out=qf[par][:, :].rearrange("p (h t) -> p h t", t=128),
                    in_=qkvc_d[0:QD2, t0:t0 + 128].rearrange("(h p) t -> p h t", p=128))

            @p.step("gpsimd", waits=list(firstwaits) + [tok["kf"][par]],
                    inc=f"d_kf{par}", k=16)
            def _(e):
                return e.dma_start(
                    out=kf[par][:, :].rearrange("p (h t) -> p h t", t=128),
                    in_=qkvc_d[QD2:2 * QD2, t0:t0 + 128].rearrange("(h p) t -> p h t", p=128))
            ld[("qf", ch)] = p.full(f"d_qf{par}")
            ld[("kf", ch)] = p.full(f"d_kf{par}")
            @p.step("sync", waits=list(firstwaits) + [tok["ktm"]], inc="d_ktm", k=16)
            def _(e, par=par):
                return e.dma_start(out=ktm2[par][:, :],
                                   in_=qkvc_d[QD2:2 * QD2, t0:t0 + 128],
                                   transpose=True)

            @p.step("sync", waits=[tok["qtm"]], inc="d_qtm", k=16)
            def _(e):
                return e.dma_start(out=qtm[:, :], in_=qkvc_d[0:QD2, t0:t0 + 128],
                                   transpose=True)

            @p.step("sync", waits=[tok["vtm"]], inc="d_vtm", k=16)
            def _(e):
                return e.dma_start(out=vtm[:, :], in_=qkvc_d[2 * QD2:QKV, t0:t0 + 128],
                                   transpose=True)
            ld[("ktm", ch)] = p.full("d_ktm")
            ld[("qtm", ch)] = p.full("d_qtm")
            ld[("vtm", ch)] = p.full("d_vtm")

            @p.step("scalar", waits=list(firstwaits) + [tok["zf"]], inc="d_zf", k=16)
            def _(e):
                return e.dma_start(
                    out=zf[:, :].rearrange("p (h t) -> p h t", t=128),
                    in_=zsil_d[:, t0:t0 + 128].rearrange("(h p) t -> p h t", p=128))
            ld[("zf", ch)] = p.full("d_zf")

        QD2 = 2048
        chunk_loads(0, firstwaits=bar + [init_tok])

        for ch in range(NCH):
            par = ch % 2
            t0 = ch * P2C
            co = 128 + t0

            @p.step("tensor",
                    waits=(bar + [init_tok]) if ch == 0 else [tok.get("ptrf_free")],
                    inc="pe2")
            def _(e, co=co, t0=t0):
                i32 = identf[0:32, 0:32]
                e.transpose(ptr_f[:, 0:32], cpad[:, co:co + 128], i32)
                e.transpose(ptr_f[:, 32:64], cpad[:, co - 1:co + 127], i32)
                e.transpose(ptr_f[:, 64:96], lnbeta[:, t0:t0 + 128], i32)
                e.transpose(ptr_f[0:1, 96:128], cpad[:, co - 1:co], i32)
                return e.transpose(ptr_f[0:1, 128:160], cpad[:, co + 127:co + 128], i32)
            trf_tok = ("pe2", p.cnt["pe2"])

            @p.step("tensor", waits=[tok.get("ptrb_free")], inc="pe2")
            def _(e, t0=t0):
                return e.transpose(ptr_b[:, 0:32], beta_b[:, t0:t0 + 128],
                                   identb[0:32, 0:32])
            trb_tok = ("pe2", p.cnt["pe2"])

            @p.step("vector", waits=[trf_tok], inc="dv2")
            def _(e):
                return e.tensor_copy(out=sm["cTt"][:, :], in_=ptr_f[:, 0:32])

            @p.step("vector", inc="dv2")
            def _(e):
                return e.tensor_copy(out=sm["cpTt"][:, :], in_=ptr_f[:, 32:64])

            @p.step("vector", inc="dv2")
            def _(e):
                return e.tensor_copy(out=sm["lnbT"][:, :], in_=ptr_f[:, 64:96])

            @p.step("vector", inc="dv2")
            def _(e):
                e.tensor_copy(out=rA[:, 0:32], in_=ptr_f[0:1, 96:128])
                return e.tensor_copy(out=rA[:, 32:64], in_=ptr_f[0:1, 128:160])

            @p.step("vector", waits=[trb_tok], inc="dv2")
            def _(e):
                return e.tensor_copy(out=sm["bTf"][:, :], in_=ptr_b[:, 0:32])

            @p.step("vector", inc="dv2")
            def _(e):
                return e.tensor_copy(out=bTb[:, :], in_=ptr_b[:, 0:32])
            prep_evac_tok = ("dv2", p.cnt["dv2"])
            tok["ptrf_free0"] = prep_evac_tok

            @p.step("tensor", waits=[prep_evac_tok], inc="pe2")
            def _(e):
                e.matmul(ptr_f[:, 160:192], onesf[:, :], rA[:, 0:32],
                         start=True, stop=True)
                return e.matmul(ptr_f[:, 192:224], onesf[:, :], rA[:, 32:64],
                                start=True, stop=True)
            rep_tok = ("pe2", p.cnt["pe2"])

            @p.step("vector", waits=[rep_tok], inc="dv2")
            def _(e):
                return e.tensor_copy(out=sm["Cp0rep"][:, :], in_=ptr_f[:, 160:192])

            @p.step("vector", inc="dv2")
            def _(e):
                return e.tensor_copy(out=sm["CCrep"][:, :], in_=ptr_f[:, 192:224])
            tok["ptrf_free"] = ("dv2", p.cnt["dv2"])

            sq_toks = []
            for h in range(NK):
                @p.step("scalar", waits=[ld[("ktm", ch)]], inc="ac2")
                def _(e, h=h, par=par):
                    return e.activation(out=trash[:, :],
                                        in_=ktm2[par][:, h * 128:(h + 1) * 128],
                                        func=AF.Square, accum_out=nk2[:, h:h + 1])
            for h in range(NK):
                @p.step("scalar", waits=[ld[("qtm", ch)]], inc="ac2")
                def _(e, h=h):
                    return e.activation(out=trash[:, :], in_=qtm[:, h * 128:(h + 1) * 128],
                                        func=AF.Square, accum_out=nq2[:, h:h + 1])
            sq_tok = ("ac2", p.cnt["ac2"])
            tok["qtm"] = sq_tok

            @p.step("vector", waits=[sq_tok], inc="dv2")
            def _(e):
                return e.tensor_scalar(out=nk2[:, :], in0=nk2[:, :], scalar1=1e-24,
                                       scalar2=None, op0=ALU.max)

            @p.step("vector", inc="dv2")
            def _(e):
                return e.tensor_scalar(out=nq2[:, :], in0=nq2[:, :], scalar1=1e-24,
                                       scalar2=None, op0=ALU.max)
            mx_tok = ("dv2", p.cnt["dv2"])

            @p.step("scalar", waits=[mx_tok], inc="ac2")
            def _(e):
                return e.activation(out=lnk[:, :], in_=nk2[:, :], func=AF.Ln)

            @p.step("scalar", inc="ac2")
            def _(e):
                return e.activation(out=lnq[:, :], in_=nq2[:, :], func=AF.Ln)
            ln_tok = ("ac2", p.cnt["ac2"])

            def lnbc(t):
                return t[:, :].rearrange("t (h o) -> t h o", o=1).broadcast_to([128, 16, 2])

            def U32(t):
                return t[:, :].rearrange("t (h o) -> t h o", o=2)

            @p.step("vector", waits=[ln_tok, prep_evac_tok], inc="dv2")
            def _(e):
                return e.scalar_tensor_tensor(out=U32(sm["U1"]), in0=lnbc(lnk),
                                              scalar=-0.5, in1=U32(sm["cpTt"]),
                                              op0=ALU.mult, op1=ALU.add)

            @p.step("vector", inc="dv2")
            def _(e):
                return e.tensor_add(out=sm["U1"][:, :], in0=sm["U1"][:, :],
                                    in1=sm["lnbT"][:, :])

            @p.step("vector", inc="dv2")
            def _(e):
                return e.scalar_tensor_tensor(out=U32(sm["U2n"]), in0=lnbc(lnk),
                                              scalar=-0.5, in1=U32(sm["cTt"]),
                                              op0=ALU.mult, op1=ALU.subtract)

            @p.step("vector", inc="dv2")
            def _(e):
                return e.scalar_tensor_tensor(out=U32(sm["U3"]), in0=lnbc(lnq),
                                              scalar=-0.5, in1=U32(sm["cTt"]),
                                              op0=ALU.mult, op1=ALU.add)

            @p.step("vector", waits=[tok["ptrf_free"]], inc="dv2")
            def _(e):
                return e.tensor_sub(out=sm["U3s"][:, :], in0=sm["U3"][:, :],
                                    in1=sm["Cp0rep"][:, :])

            @p.step("vector", inc="dv2")
            def _(e):
                return e.tensor_sub(out=sm["targ"][:, :], in0=sm["U1"][:, :],
                                    in1=sm["Cp0rep"][:, :])
            u1s_tok = ("dv2", p.cnt["dv2"])

            @p.step("scalar", waits=[u1s_tok], inc="ac2")
            def _(e):
                return e.activation(out=sm["negeU1s"][:, :], in_=sm["targ"][:, :],
                                    func=AF.Exp)
            expu_tok = ("ac2", p.cnt["ac2"])

            @p.step("vector", waits=[expu_tok], inc="dv2")
            def _(e):
                return e.tensor_scalar(out=sm["negeU1s"][:, :], in0=sm["negeU1s"][:, :],
                                       scalar1=-1.0, scalar2=None, op0=ALU.mult)
            negu_tok = ("dv2", p.cnt["dv2"])

            @p.step("vector", inc="dv2")
            def _(e):
                return e.tensor_add(out=sm["Edec"][:, :], in0=sm["U2n"][:, :],
                                    in1=sm["CCrep"][:, :])

            @p.step("vector", inc="dv2")
            def _(e):
                return e.tensor_sub(out=sm["Sdec"][:, :], in0=sm["CCrep"][:, :],
                                    in1=sm["Cp0rep"][:, :])
            dec_tok = ("dv2", p.cnt["dv2"])

            @p.step("scalar", waits=[dec_tok], inc="ac2")
            def _(e):
                return e.activation(out=sm["Edec"][:, :], in_=sm["Edec"][:, :], func=AF.Exp)

            @p.step("scalar", inc="ac2")
            def _(e):
                return e.activation(out=sm["Sdec"][:, :], in_=sm["Sdec"][:, :], func=AF.Exp)
            dece_tok = ("ac2", p.cnt["ac2"])

            split_done = []
            for ci, name in enumerate(("U1", "U2n", "U3", "U3s")):
                @p.step("vector", waits=[tok.get("dvb_free")] if ci == 0 else [],
                        inc="dv2")
                def _(e, ci=ci, name=name):
                    return e.tensor_copy(out=Usp[:, ci * 128:ci * 128 + 32],
                                         in_=sm[name][:, :])

                @p.step("vector", inc="dv2")
                def _(e, ci=ci, name=name):
                    return e.tensor_sub(out=Ur1[:, :], in0=sm[name][:, :],
                                        in1=Usp[:, ci * 128:ci * 128 + 32])

                @p.step("vector", inc="dv2")
                def _(e, ci=ci):
                    return e.tensor_copy(out=Usp[:, ci * 128 + 32:ci * 128 + 64],
                                         in_=Ur1[:, :])

                @p.step("vector", inc="dv2")
                def _(e, ci=ci):
                    return e.tensor_sub(out=Ur2[:, :], in0=Ur1[:, :],
                                        in1=Usp[:, ci * 128 + 32:ci * 128 + 64])

                @p.step("vector", inc="dv2")
                def _(e, ci=ci):
                    return e.tensor_copy(out=Usp[:, ci * 128 + 64:ci * 128 + 96],
                                         in_=Ur2[:, :])
            spl_tok = ("dv2", p.cnt["dv2"])

            @p.step("tensor", waits=[spl_tok, prep_evac_tok], inc="pe2")
            def _(e):
                inst = None
                for ci in range(4):
                    inst = e.transpose(ptr_b[:, ci * 128:(ci + 1) * 128],
                                       Usp[:, ci * 128:(ci + 1) * 128], identb[:, :])
                return inst
            utr_tok = ("pe2", p.cnt["pe2"])

            @p.step("vector", waits=[utr_tok, tok.get("xkb_free")], inc="dv2")
            def _(e):
                return e.tensor_copy(out=Utr[:, 0:512], in_=ptr_b[:, 0:512])
            utr_ev = ("dv2", p.cnt["dv2"])
            tok["ptrb_free"] = utr_ev

            for (stg, ci, lo) in ((LU1, 0, 0), (RU2, 1, 3),
                                  (RU1, 0, 3), (RU3, 2, 3)):
                @p.step("sync", waits=[utr_ev, tok["stg"]], inc="d_fl", k=16)
                def _(e, stg=stg, ci=ci, lo=lo):
                    return e.dma_start(
                        out=stg[lo:lo + 3, :],
                        in_=Utr[0:96, ci * 128:(ci + 1) * 128])
            fl_tok = p.full("d_fl")

            gk_toks = {}
            pe_free = {0: tok.get("pe_a_free"), 1: tok.get("pe_b_free")}
            for h in range(NK):
                bank = [pe_a, pe_b][h % 2]
                @p.step("tensor", waits=[ld[("kf", ch)], ld[("qf", ch)],
                                         pe_free[h % 2]], inc="pe2")
                def _(e, h=h, bank=bank, par=par):
                    e.matmul(bank[:, 0:128], kf[par][:, h * 128:(h + 1) * 128],
                             kf[par][:, h * 128:(h + 1) * 128], start=True, stop=True)
                    return e.matmul(bank[:, 128:256], kf[par][:, h * 128:(h + 1) * 128],
                                    qf[par][:, h * 128:(h + 1) * 128],
                                    start=True, stop=True)
                mm_tok = ("pe2", p.cnt["pe2"])

                @p.step("scalar", waits=[mm_tok], inc="ac2")
                def _(e, h=h, bank=bank):
                    return e.copy(out=Gl[:, h * 128:(h + 1) * 128], in_=bank[:, 0:128])

                @p.step("scalar", inc="ac2")
                def _(e, h=h, bank=bank):
                    return e.copy(out=Qu[:, h * 128:(h + 1) * 128], in_=bank[:, 128:256])
                gev = ("ac2", p.cnt["ac2"])
                pe_free[h % 2] = gev

                @p.step("gpsimd", waits=[gev], inc="po2")
                def _(e, h=h):
                    return e.tensor_mul(out=Gu[:, h * 128:(h + 1) * 128],
                                        in0=Gl[:, h * 128:(h + 1) * 128],
                                        in1=maskUs[:, :])

                @p.step("gpsimd", inc="po2")
                def _(e, h=h):
                    return e.tensor_mul(out=Qu[:, h * 128:(h + 1) * 128],
                                        in0=Qu[:, h * 128:(h + 1) * 128],
                                        in1=maskUi[:, :])

                @p.step("gpsimd", inc="po2")
                def _(e, h=h):
                    return e.tensor_mul(out=Gl[:, h * 128:(h + 1) * 128],
                                        in0=Gl[:, h * 128:(h + 1) * 128],
                                        in1=maskL[:, :])
            gmask_tok = ("po2", p.cnt["po2"])

            etypes = [(Aw, LU1, RU2, Gl), (ATw, LU2, RU1, Gu),
                      (Wow, LU2, RU3, Qu), (R3w, None, RU3, None)]
            eidx = 0
            etmp_free = [None, None]
            for (wide, lst, rst, comb) in etypes:
                if wide is ATw:
                    @p.step("sync", waits=[("pe2", p.cnt["pe2"]), utr_ev],
                            inc="d_fl", k=16)
                    def _(e):
                        return e.dma_start(out=LU1[0:3, :],
                                           in_=Utr[0:96, 1 * 128:2 * 128])
                    fl_tok = p.full("d_fl")
                if wide is R3w:
                    @p.step("sync", waits=[("pe2", p.cnt["pe2"]), utr_ev],
                            inc="d_fl", k=16)
                    def _(e):
                        return e.dma_start(out=RU3[3:6, :],
                                           in_=Utr[0:96, 3 * 128:4 * 128])
                    fl_tok = p.full("d_fl")
                for gq in range(8):
                    bank = [pe_a, pe_b][gq % 2]
                    Etmp = Etmp2[eidx % 2]
                    @p.step("tensor", waits=[fl_tok, lz_tok, pe_free[gq % 2]],
                            inc="pe2")
                    def _(e, wide=wide, lst=lst, rst=rst, gq=gq, bank=bank):
                        inst = None
                        for i in range(4):
                            hp = gq * 4 + i
                            lhsT = Lz[:, :] if lst is None else lst[:, hp * 128:(hp + 1) * 128]
                            inst = e.matmul(bank[:, i * 128:(i + 1) * 128], lhsT,
                                            rst[:, hp * 128:(hp + 1) * 128],
                                            start=True, stop=True)
                        return inst
                    emm_tok = ("pe2", p.cnt["pe2"])

                    if eidx % 2 == 0:
                        @p.step("vector", waits=[emm_tok, etmp_free[eidx % 2]],
                                inc="dv2")
                        def _(e, bank=bank, Etmp=Etmp):
                            return e.tensor_scalar(out=Etmp[:, :], in0=bank[:, :],
                                                   scalar1=CL, scalar2=None,
                                                   op0=ALU.min)
                        cl_tok = ("dv2", p.cnt["dv2"])
                        pe_free[gq % 2] = cl_tok

                        @p.step("scalar", waits=[cl_tok], inc="ac2")
                        def _(e, wide=wide, gq=gq, Etmp=Etmp):
                            return e.activation(out=wide[:, gq * 512:(gq + 1) * 512],
                                                in_=Etmp[:, :], func=AF.Exp)
                        ex_tok = ("ac2", p.cnt["ac2"])
                    else:
                        @p.step("scalar", waits=[emm_tok, etmp_free[eidx % 2]],
                                inc="ac2")
                        def _(e, bank=bank, Etmp=Etmp):
                            return e.activation(out=Etmp[:, :], in_=bank[:, :],
                                                func=AF.Relu, scale=-1.0,
                                                bias=cl60[:, :])
                        cl_tok = ("ac2", p.cnt["ac2"])
                        pe_free[gq % 2] = cl_tok

                        @p.step("scalar", inc="ac2")
                        def _(e, wide=wide, gq=gq, Etmp=Etmp):
                            return e.activation(out=wide[:, gq * 512:(gq + 1) * 512],
                                                in_=Etmp[:, :], func=AF.Exp,
                                                scale=-1.0, bias=cl60[:, :])
                        ex_tok = ("ac2", p.cnt["ac2"])
                    etmp_free[eidx % 2] = ex_tok
                    eidx += 1

                    if comb is not None:
                        @p.step("vector", waits=[ex_tok, gmask_tok], inc="dv2")
                        def _(e, wide=wide, comb=comb, gq=gq):
                            cb = comb[:, gq * 256:(gq + 1) * 256] \
                                .rearrange("p (h o t) -> p h o t", o=1, t=128) \
                                .broadcast_to([128, 2, 2, 128])
                            return e.tensor_tensor(
                                out=wide[:, gq * 512:(gq + 1) * 512]
                                    .rearrange("p (h o t) -> p h o t", o=2, t=128),
                                in0=wide[:, gq * 512:(gq + 1) * 512]
                                    .rearrange("p (h o t) -> p h o t", o=2, t=128),
                                in1=cb, op=ALU.mult)
            ew_tok = ("dv2", p.cnt["dv2"])
            ew_ac = ("ac2", p.cnt["ac2"])
            tok["stg"] = ("pe2", p.cnt["pe2"])

            banks = [px, ptr_f, py, ps_a, ps_b, pe_a, pe_b]
            bankf = tok.get("bankf", {id(b): None for b in banks})
            bidx = tok.get("bidx", 0)

            def next_bank():
                nonlocal bidx
                b = banks[bidx % len(banks)]
                bidx += 1
                return b

            wv = tok.get("wv", {})

            u_toks = []
            for gq in range(8):
                b = next_bank()
                h0, h1 = 2 * gq, 2 * gq + 1
                @p.step("tensor", waits=[ld[("kf", ch)], ew_tok, ew_ac,
                                         bankf[id(b)], tok["Sb_w"],
                                         wv.get("xkbw_last")], inc="pe2")
                def _(e, h0=h0, h1=h1, par=par, b=b):
                    e.matmul(b[:, 0:256], kf[par][:, h0 * 128:(h0 + 1) * 128],
                             Sb[:, h0 * 256:(h0 + 1) * 256], start=True, stop=True)
                    return e.matmul(b[:, 256:512],
                                    kf[par][:, h1 * 128:(h1 + 1) * 128],
                                    Sb[:, h1 * 256:(h1 + 1) * 256],
                                    start=True, stop=True)
                t_pe = ("pe2", p.cnt["pe2"])

                @p.step("vector", waits=[ld[("vtm", ch)], prep_evac_tok], inc="dv2")
                def _(e, gq=gq):
                    inst = None
                    for i in range(4):
                        hp = gq * 4 + i
                        inst = e.tensor_scalar(
                            out=ubv[:, i * 128:(i + 1) * 128],
                            in0=vtm[:, hp * 128:(hp + 1) * 128],
                            scalar1=sm["bTf"][:, hp:hp + 1], scalar2=None,
                            op0=ALU.mult)
                    return inst

                @p.step("vector", waits=[t_pe, negu_tok], inc="dv2")
                def _(e, gq=gq, b=b):
                    inst = None
                    for i in range(4):
                        hp = gq * 4 + i
                        inst = e.scalar_tensor_tensor(
                            out=xkbw[:, gq * 512 + i * 128:gq * 512 + (i + 1) * 128],
                            in0=b[:, i * 128:(i + 1) * 128],
                            scalar=sm["negeU1s"][:, hp:hp + 1],
                            in1=ubv[:, i * 128:(i + 1) * 128],
                            op0=ALU.mult, op1=ALU.add)
                    return inst
                bankf[id(b)] = ("dv2", p.cnt["dv2"])
                u_toks.append(bankf[id(b)])

            tok["kf"][0] = tok["kf"][1] = ("pe2", p.cnt["pe2"])
            tok["vtm"] = ("dv2", p.cnt["dv2"])
            ap_tok = []
            for gq in range(8):
                b = next_bank()
                @p.step("tensor", waits=[u_toks[gq], bankf[id(b)]], inc="pe2")
                def _(e, gq=gq, b=b):
                    inst = None
                    for i in range(4):
                        hp = gq * 4 + i
                        inst = e.matmul(b[:, i * 128:(i + 1) * 128],
                                        ATw[:, hp * 128:(hp + 1) * 128],
                                        xkbw[:, gq * 512 + i * 128:gq * 512 + (i + 1) * 128],
                                        start=True, stop=True)
                    return inst
                t_pe = ("pe2", p.cnt["pe2"])

                @p.step("vector", waits=[t_pe], inc="dv2")
                def _(e, gq=gq, b=b):
                    s = xkbw[:, gq * 512:(gq + 1) * 512]
                    return e.tensor_add(out=s, in0=b[:, :], in1=s)
                bankf[id(b)] = ("dv2", p.cnt["dv2"])
                ap_tok.append(bankf[id(b)])

            Aping = [Aw, Aw2]
            Tping = [ATw, ATw2]
            sA_prev = [None] * 8
            sT_prev = [None] * 8
            for lev in range(1, 7):
                half = lev % 2
                sA_t = []
                sT_t = []
                for gq in range(8):
                    bA = next_bank() if lev < 6 else None
                    bT = next_bank()
                    @p.step("tensor", waits=[sA_prev[gq], sT_prev[gq],
                                             None if bA is None else bankf[id(bA)],
                                             bankf[id(bT)]] +
                                            ([ap_tok[gq]] if lev == 1 else []),
                            inc="pe2")
                    def _(e, gq=gq, lev=lev, bA=bA, bT=bT, half=half):
                        inst = None
                        for i in range(4):
                            hp = gq * 4 + i
                            sl = slice(i * 128, (i + 1) * 128)
                            if lev == 1:
                                a = Aw[:, hp * 128:(hp + 1) * 128]
                                at = ATw[:, hp * 128:(hp + 1) * 128]
                            else:
                                ph = (lev - 1) % 2
                                a = Aping[ph][:, hp * 128:(hp + 1) * 128]
                                at = Tping[ph][:, hp * 128:(hp + 1) * 128]
                            if bA is not None:
                                e.matmul(bA[:, sl], at, a, start=True, stop=True)
                            inst = e.matmul(bT[:, sl], a, at, start=True, stop=True)
                        return inst
                    t_sq = ("pe2", p.cnt["pe2"])

                    if bA is not None:
                        @p.step("scalar", waits=[t_sq], inc="ac2")
                        def _(e, gq=gq, bA=bA, half=half):
                            return e.copy(out=Aping[half][:, gq * 512:(gq + 1) * 512],
                                          in_=bA[:, :])
                        sA_t.append(("ac2", p.cnt["ac2"]))
                        bankf[id(bA)] = sA_t[-1]
                    else:
                        sA_t.append(None)
                    @p.step("vector", waits=[t_sq], inc="dv2")
                    def _(e, gq=gq, bT=bT, half=half):
                        return e.tensor_copy(out=Tping[half][:, gq * 512:(gq + 1) * 512],
                                             in_=bT[:, :])
                    sT_t.append(("dv2", p.cnt["dv2"]))
                    bankf[id(bT)] = sT_t[-1]
                sA_prev = sA_t
                sT_prev = sT_t

                napt = []
                for gq in range(8):
                    b = next_bank()
                    use_ident = (gq % 2 == 1)
                    @p.step("tensor", waits=[sT_t[gq], ap_tok[gq], bankf[id(b)]],
                            inc="pe2")
                    def _(e, gq=gq, b=b, half=half, use_ident=use_ident):
                        inst = None
                        for i in range(4):
                            hp = gq * 4 + i
                            sl = slice(i * 128, (i + 1) * 128)
                            xs = xkbw[:, gq * 512 + i * 128:gq * 512 + (i + 1) * 128]
                            if use_ident:
                                e.matmul(b[:, sl], identb[:, :], xs,
                                         start=True, stop=False)
                                inst = e.matmul(b[:, sl],
                                                Tping[half][:, hp * 128:(hp + 1) * 128],
                                                xs, start=False, stop=True)
                            else:
                                inst = e.matmul(b[:, sl],
                                                Tping[half][:, hp * 128:(hp + 1) * 128],
                                                xs, start=True, stop=True)
                        return inst
                    t_ap = ("pe2", p.cnt["pe2"])

                    if use_ident:
                        @p.step("scalar", waits=[t_ap], inc="ac2")
                        def _(e, gq=gq, b=b):
                            return e.copy(out=xkbw[:, gq * 512:(gq + 1) * 512],
                                          in_=b[:, :])
                        bankf[id(b)] = ("ac2", p.cnt["ac2"])
                    else:
                        @p.step("vector", waits=[t_ap], inc="dv2")
                        def _(e, gq=gq, b=b):
                            s = xkbw[:, gq * 512:(gq + 1) * 512]
                            return e.tensor_add(out=s, in0=b[:, :], in1=s)
                        bankf[id(b)] = ("dv2", p.cnt["dv2"])
                    napt.append(bankf[id(b)])
                ap_tok = napt

            og_done_g = None
            o_toks = []
            pend = []
            for gq in range(8):
                b = next_bank()
                qscL = Lqsc[gq % 2]
                obfL = Lobf[gq % 2]
                rofL, robL = Lrof[gq % 2], Lrob[gq % 2]
                @p.step("vector", waits=[ew_tok, ew_ac, ld[("qf", ch)],
                                         wv.get(("oq", gq % 2))], inc="dv2")
                def _(e, gq=gq, par=par, qscL=qscL):
                    qb = qf[par][:, 2 * gq * 128:(2 * gq + 2) * 128] \
                        .rearrange("p (h o t) -> p h o t", o=1, t=128) \
                        .broadcast_to([128, 2, 2, 128])
                    return e.tensor_tensor(
                        out=qscL[:, :].rearrange("p (h o t) -> p h o t", o=2, t=128),
                        in0=qb,
                        in1=R3w[:, gq * 512:(gq + 1) * 512]
                            .rearrange("p (h o t) -> p h o t", o=2, t=128),
                        op=ALU.mult)
                t_qsc = ("dv2", p.cnt["dv2"])

                @p.step("tensor", waits=[t_qsc, ap_tok[gq], bankf[id(b)]], inc="pe2")
                def _(e, gq=gq, b=b, qscL=qscL):
                    inst = None
                    for i in range(4):
                        hp = gq * 4 + i
                        h, pp = hp // 2, hp % 2
                        sl = slice(i * 128, (i + 1) * 128)
                        e.matmul(b[:, sl],
                                 xkbw[:, gq * 512 + i * 128:gq * 512 + (i + 1) * 128],
                                 Wow[:, hp * 128:(hp + 1) * 128],
                                 start=True, stop=False)
                        inst = e.matmul(
                            b[:, sl],
                            Sb[:, h * 256 + pp * 128:h * 256 + (pp + 1) * 128],
                            qscL[:, sl], start=False, stop=True)
                    return inst
                t_omm = ("pe2", p.cnt["pe2"])

                @p.step("scalar", waits=[t_omm], inc="ac2")
                def _(e, b=b, obfL=obfL):
                    return e.copy(out=obfL[:, :], in_=b[:, :])
                t_obf = ("ac2", p.cnt["ac2"])

                @p.step("vector", waits=[t_obf], inc="dv2")
                def _(e, obfL=obfL, qscL=qscL):
                    return e.tensor_mul(out=qscL[:, :], in0=obfL[:, :], in1=obfL[:, :])
                t_osq = ("dv2", p.cnt["dv2"])

                @p.step("tensor", waits=[t_osq], inc="pe2")
                def _(e, b=b, qscL=qscL):
                    return e.matmul(b[0:1, 0:512], onescb[:, :], qscL[:, :],
                                    start=True, stop=True)
                t_row = ("pe2", p.cnt["pe2"])

                @p.step("vector", waits=[t_row], inc="dv2")
                def _(e, b=b, rofL=rofL):
                    return e.tensor_scalar(out=rofL[:, :], in0=b[0:1, 0:512],
                                           scalar1=1.0 / 128, scalar2=EPS_RMS,
                                           op0=ALU.mult, op1=ALU.add)
                t_ro = ("dv2", p.cnt["dv2"])

                @p.step("scalar", waits=[t_ro], inc="ac2")
                def _(e, rofL=rofL):
                    return e.activation(out=rofL[:, :], in_=rofL[:, :], func=AF.Ln)

                @p.step("scalar", inc="ac2")
                def _(e, rofL=rofL):
                    return e.activation(out=rofL[:, :], in_=rofL[:, :],
                                        func=AF.Exp, scale=-0.5)
                t_rl = ("ac2", p.cnt["ac2"])

                @p.step("vector", waits=[t_rl], inc="dv2")
                def _(e, rofL=rofL, robL=robL):
                    return e.tensor_copy(out=robL[:, :], in_=rofL[:, :])
                t_rb = ("dv2", p.cnt["dv2"])

                @p.step("tensor", waits=[t_rb, t_ro], inc="pe2")
                def _(e, b=b, robL=robL):
                    return e.matmul(b[:, 0:512], onesb[:, :], robL[:, :],
                                    start=True, stop=True)
                t_rrep = ("pe2", p.cnt["pe2"])

                @p.step("vector", waits=[t_rrep, ld[("zf", ch)]], inc="dv2")
                def _(e, b=b, obfL=obfL):
                    return e.tensor_mul(out=obfL[:, :], in0=obfL[:, :], in1=b[:, 0:512])
                @p.step("vector", inc="dv2")
                def _(e, obfL=obfL):
                    return e.tensor_scalar(out=obfL[:, :], in0=obfL[:, :],
                                           scalar1=normw[:, :], scalar2=None,
                                           op0=ALU.mult)

                @p.step("vector", waits=[tok["ogb"][par]], inc="dv2")
                def _(e, gq=gq, par=par, obfL=obfL):
                    return e.tensor_mul(out=ogb[par][:, gq * 512:(gq + 1) * 512],
                                        in0=obfL[:, :],
                                        in1=zf[:, gq * 512:(gq + 1) * 512])
                og_done_g = ("dv2", p.cnt["dv2"])
                bankf[id(b)] = og_done_g
                wv[("oq", gq % 2)] = og_done_g
                o_toks.append(og_done_g)

            for gq in range(8):
                b = next_bank()
                kdeL = Lkde[gq % 2]
                @p.step("vector", waits=[dece_tok, ld[("ktm", ch)],
                                         wv.get(("sk", gq % 2))], inc="dv2")
                def _(e, gq=gq, kdeL=kdeL, par=par):
                    kb = ktm2[par][:, 2 * gq * 128:(2 * gq + 2) * 128] \
                        .rearrange("p (h o t) -> p h o t", o=1, t=128) \
                        .broadcast_to([128, 2, 2, 128])
                    eb = sm["Edec"][:, 4 * gq:4 * gq + 4] \
                        .rearrange("p (hp o) -> p hp o", o=1) \
                        .broadcast_to([128, 4, 128]) \
                        .rearrange("p (h pp) t -> p h pp t", pp=2)
                    return e.tensor_tensor(
                        out=kdeL[:, :].rearrange("p (h o t) -> p h o t", o=2, t=128),
                        in0=kb, in1=eb, op=ALU.mult)
                t_kd = ("dv2", p.cnt["dv2"])

                @p.step("tensor", waits=[t_kd, o_toks[gq], bankf[id(b)]], inc="pe2")
                def _(e, gq=gq, b=b, kdeL=kdeL):
                    inst = None
                    for i in range(4):
                        sl = slice(i * 128, (i + 1) * 128)
                        inst = e.matmul(b[:, sl], kdeL[:, sl],
                                        xkbw[:, gq * 512 + i * 128:gq * 512 + (i + 1) * 128],
                                        start=True, stop=True)
                    return inst
                t_smm = ("pe2", p.cnt["pe2"])
                wv[("sk", gq % 2)] = t_smm

                @p.step("vector", waits=[t_smm], inc="dv2")
                def _(e, gq=gq, b=b):
                    inst = None
                    for i in range(4):
                        hp = gq * 4 + i
                        h, pp = hp // 2, hp % 2
                        ssl = slice(h * 256 + pp * 128, h * 256 + (pp + 1) * 128)
                        inst = e.scalar_tensor_tensor(
                            out=Sb[:, ssl], in0=Sb[:, ssl],
                            scalar=sm["Sdec"][:, hp:hp + 1],
                            in1=b[:, i * 128:(i + 1) * 128],
                            op0=ALU.mult, op1=ALU.add)
                    return inst
                t_sev = ("dv2", p.cnt["dv2"])
                bankf[id(b)] = t_sev
                tok["Sb_w"] = t_sev
            wv["xkbw_last"] = ("pe2", p.cnt["pe2"])
            tok["wv"] = wv
            tok["bankf"] = bankf
            tok["bidx"] = bidx
            tok["dvb_free"] = ("pe2", p.cnt["pe2"])
            tok["xkb_free"] = ("pe2", p.cnt["pe2"])
            tok["ptrf_free"] = bankf[id(ptr_f)]
            tok["pe_a_free"] = bankf[id(pe_a)]
            tok["pe_b_free"] = bankf[id(pe_b)]

            tok["ktm"] = ("dv2", p.cnt["dv2"])
            tok["zf"] = ("dv2", p.cnt["dv2"])
            tok["qf"][par] = ("dv2", p.cnt["dv2"])

            @p.step("sync", waits=[og_done_g], inc="d_og", k=16)
            def _(e, par=par, t0=t0):
                return e.dma_start(
                    out=og_d[:, t0:t0 + 128].rearrange("(hp p) t -> p hp t", p=128),
                    in_=ogb[par][:, :].rearrange("p (hp t) -> p hp t", t=128))
            tok["ogb"][0] = tok["ogb"][1] = p.full("d_og")

            if ch + 1 < NCH:
                chunk_loads(ch + 1)

        g["p2_bar"] = [("pe2", p.cnt["pe2"]), ("dv2", p.cnt["dv2"]),
                       ("ac2", p.cnt["ac2"]), p.full("d_og")]



def emit_p25(nc, p, ctx, env):
    g = env
    og_d, x1f_d, xT_d, woutT = g["og_d"], g["x1f_d"], g["xT"], g["woutT"]
    onesf, onescb = g["onesf"], g["onescb"]
    bar = g["p2_bar"]

    sc = ExitStack()
    with sc:
        ee = sc.enter_context
        ogr = ee(nc.sbuf_tensor("ogr", [128, NFZ * T], BF16))
        wo = [ee(nc.sbuf_tensor(f"wo{i}", [128, NFZ * 128], BF16)) for i in range(2)]
        xb = [ee(nc.sbuf_tensor(f"xb{i}", [128, 512], F32)) for i in range(2)]
        rt2 = [ee(nc.sbuf_tensor(f"rt{i}", [128, 512], F32)) for i in range(2)]
        rb = ee(nc.sbuf_tensor("rb", [128, 512], BF16))
        rsq = ee(nc.sbuf_tensor("rsq", [128, 512], BF16))
        srow = ee(nc.sbuf_tensor("srow", [1, T], F32))
        qrow_ = ee(nc.sbuf_tensor("qrow_", [1, T], F32))
        mrow = ee(nc.sbuf_tensor("mrow", [1, 512], F32))
        vrow = ee(nc.sbuf_tensor("vrow", [1, 512], F32))
        rpass = [ee(nc.sbuf_tensor(f"rpass{i}", [128, 512], F32)) for i in range(2)]
        py_ = [ee(nc.psum_tensor(f"p25y{i}", [128, 512], F32)) for i in range(2)]
        pst = ee(nc.psum_tensor("pst", [1, 1024], F32))
        prp1 = ee(nc.psum_tensor("prp1", [128, 512], F32))
        prp2 = ee(nc.psum_tensor("prp2", [128, 512], F32))

        for q in range(4):
            @p.step("gpsimd", waits=bar if q == 0 else [], inc="d_ogr", k=16)
            def _(e, q=q):
                return e.dma_start(
                    out=ogr[:, :].rearrange("p (v t) -> p v t", t=T)[:, q * 8:(q + 1) * 8],
                    in_=og_d[q * 1024:(q + 1) * 1024, :]
                        .rearrange("(v p) t -> p v t", p=128))
        ogr_tok = p.full("d_ogr")

        @p.step("vector", waits=bar, inc="dv3")
        def _(e):
            return e.memset(srow[:, :], 0.0)

        @p.step("vector", inc="dv3")
        def _(e):
            return e.memset(qrow_[:, :], 0.0)
        srow_z = ("dv3", p.cnt["dv3"])

        wol_tok = [None, None]
        wof_tok = [None, None]
        xf_tok = [None, None]
        pyf = [None, None]
        pst_free = None

        def wo_load(cc):
            i = cc % 2
            @p.step("gpsimd", waits=[wof_tok[i]] + (bar if cc == 0 else []),
                    inc=f"d_wo{i}", k=16)
            def _(e, cc=cc, i=i):
                return e.dma_start(
                    out=wo[i][:, :].rearrange("p (v c) -> p v c", c=128),
                    in_=woutT[:, cc * 128:(cc + 1) * 128]
                        .rearrange("(v p) c -> p v c", p=128))
            wol_tok[i] = p.full(f"d_wo{i}")

        wo_load(0)
        wo_load(1)

        for cc in range(NCC):
            i = cc % 2
            for ts in range(4):
                bank = py_[ts % 2]
                @p.step("scalar", waits=[xf_tok[i]] + (bar if cc < 2 and ts == 0 else []),
                        inc=f"d_xr{i}", k=16)
                def _(e, cc=cc, ts=ts, i=i):
                    return e.dma_start(out=xb[i][:, :],
                                       in_=xT_d[cc * 128:(cc + 1) * 128,
                                                ts * 512:(ts + 1) * 512])
                xld = p.full(f"d_xr{i}")

                @p.step("tensor", waits=[wol_tok[i], ogr_tok, pyf[ts % 2]],
                        inc="pe3")
                def _(e, cc=cc, ts=ts, i=i, bank=bank):
                    inst = None
                    for v in range(NFZ):
                        inst = e.matmul(bank[:, :], wo[i][:, v * 128:(v + 1) * 128],
                                        ogr[:, v * T + ts * 512:v * T + (ts + 1) * 512],
                                        start=(v == 0), stop=(v == NFZ - 1))
                    return inst
                ymm = ("pe3", p.cnt["pe3"])
                if ts == 3:
                    wof_tok[i] = ymm
                    if cc + 2 < NCC:
                        wo_load(cc + 2)

                rpar = (cc * 4 + ts) % 2
                rt = rt2[rpar]
                @p.step("vector", waits=[ymm, xld, p.full(f"d_r{rpar}")], inc="dv3")
                def _(e, i=i, bank=bank, rt=rt):
                    return e.tensor_add(out=rt[:, :], in0=bank[:, :], in1=xb[i][:, :])
                radd = ("dv3", p.cnt["dv3"])
                pyf[ts % 2] = radd
                xf_tok[i] = radd

                @p.step("sync", waits=[radd], inc=f"d_r{rpar}", k=16)
                def _(e, cc=cc, ts=ts, rt=rt):
                    return e.dma_start(out=x1f_d[cc * 128:(cc + 1) * 128,
                                                 ts * 512:(ts + 1) * 512],
                                       in_=rt[:, :])

                @p.step("vector", inc="dv3")
                def _(e, rt=rt):
                    return e.tensor_copy(out=rb[:, :], in_=rt[:, :])

                @p.step("vector", inc="dv3")
                def _(e):
                    return e.tensor_mul(out=rsq[:, :], in0=rb[:, :], in1=rb[:, :])
                rsq_tok = ("dv3", p.cnt["dv3"])

                @p.step("tensor", waits=[rsq_tok, pst_free], inc="pe3")
                def _(e):
                    e.matmul(pst[:, 0:512], onescb[:, :], rb[:, :],
                             start=True, stop=True)
                    return e.matmul(pst[:, 512:1024], onescb[:, :], rsq[:, :],
                                    start=True, stop=True)
                srow_mm = ("pe3", p.cnt["pe3"])

                @p.step("vector", waits=[srow_mm, srow_z], inc="dv3")
                def _(e, ts=ts):
                    return e.tensor_add(out=srow[:, ts * 512:(ts + 1) * 512],
                                        in0=srow[:, ts * 512:(ts + 1) * 512],
                                        in1=pst[0:1, 0:512])

                @p.step("vector", inc="dv3")
                def _(e, ts=ts):
                    return e.tensor_add(out=qrow_[:, ts * 512:(ts + 1) * 512],
                                        in0=qrow_[:, ts * 512:(ts + 1) * 512],
                                        in1=pst[0:1, 512:1024])
                pst_free = ("dv3", p.cnt["dv3"])
        passA_dv = ("dv3", p.cnt["dv3"])
        passA_r = [p.full("d_r0"), p.full("d_r1")]

        rp_free = [None, None]
        rp_ld = [None, None]
        prp_free = None
        for ts in range(4):
            @p.step("vector", waits=[passA_dv], inc="dv3")
            def _(e, ts=ts):
                return e.tensor_scalar(out=mrow[:, :], in0=srow[0:1, ts * 512:(ts + 1) * 512],
                                       scalar1=1.0 / HID, scalar2=None, op0=ALU.mult)

            @p.step("vector", inc="dv3")
            def _(e, ts=ts):
                return e.tensor_scalar(out=vrow[:, :], in0=qrow_[0:1, ts * 512:(ts + 1) * 512],
                                       scalar1=1.0 / HID, scalar2=None, op0=ALU.mult)

            @p.step("vector", inc="dv3")
            def _(e):
                return e.tensor_mul(out=rb[0:1, 0:512], in0=mrow[:, :], in1=mrow[:, :])

            @p.step("vector", inc="dv3")
            def _(e):
                return e.tensor_sub(out=vrow[:, :], in0=vrow[:, :], in1=rb[0:1, 0:512])

            @p.step("vector", inc="dv3")
            def _(e):
                return e.tensor_scalar(out=vrow[:, :], in0=vrow[:, :],
                                       scalar1=LN_EPS, scalar2=None, op0=ALU.add)
            var_tok = ("dv3", p.cnt["dv3"])

            @p.step("scalar", waits=[var_tok], inc="ac3")
            def _(e):
                return e.activation(out=vrow[:, :], in_=vrow[:, :], func=AF.Ln)

            @p.step("scalar", inc="ac3")
            def _(e):
                return e.activation(out=vrow[:, :], in_=vrow[:, :], func=AF.Exp,
                                    scale=-0.5)
            rstd_tok = ("ac3", p.cnt["ac3"])

            @p.step("tensor", waits=[rstd_tok, prp_free], inc="pe3")
            def _(e):
                e.matmul(prp1[:, :], onesf[:, :], mrow[:, :], start=True, stop=True)
                return e.matmul(prp2[:, :], onesf[:, :], vrow[:, :],
                                start=True, stop=True)
            rep_tok = ("pe3", p.cnt["pe3"])

            for cc in range(NCC):
                i = cc % 2
                @p.step("scalar", waits=passA_r + [rp_free[i]], inc=f"d_rp{i}", k=16)
                def _(e, cc=cc, ts=ts, i=i):
                    return e.dma_start(out=rpass[i][:, :],
                                       in_=x1f_d[cc * 128:(cc + 1) * 128,
                                                 ts * 512:(ts + 1) * 512])
                rld = p.full(f"d_rp{i}")

                @p.step("vector", waits=[rld, rep_tok], inc="dv3")
                def _(e, i=i):
                    return e.tensor_sub(out=rpass[i][:, :], in0=rpass[i][:, :],
                                        in1=prp1[:, :])

                @p.step("vector", inc="dv3")
                def _(e, i=i):
                    return e.tensor_mul(out=rpass[i][:, :], in0=rpass[i][:, :],
                                        in1=prp2[:, :])
                x1done = ("dv3", p.cnt["dv3"])
                prp_free = x1done

                @p.step("sync", waits=[x1done], inc=f"d_x1{i}", k=16)
                def _(e, cc=cc, ts=ts, i=i):
                    return e.dma_start(out=x1f_d[cc * 128:(cc + 1) * 128,
                                                 ts * 512:(ts + 1) * 512],
                                       in_=rpass[i][:, :])
                rp_free[i] = p.full(f"d_x1{i}")

        g["p25_bar"] = [("pe3", p.cnt["pe3"]), ("dv3", p.cnt["dv3"]),
                        ("ac3", p.cnt["ac3"]), p.full("d_x10"), p.full("d_x11")]


def emit_p34(nc, p, ctx, env):
    g = env
    x1f_d, upT, downT, outT_d = g["x1f_d"], g["upT"], g["downT"], g["outT"]
    upsc, dssc, dseff, oscal = g["upsc"], g["dssc"], g["dseff"], g["oscal"]
    onesf, onescb = g["onesf"], g["onescb"]
    bar = g["p25_bar"]
    TH = T // 2

    sc = ExitStack()
    with sc:
        ee = sc.enter_context
        x1b = ee(nc.sbuf_tensor("x1b", [128, NCC * TH], BF16))
        y2 = ee(nc.sbuf_tensor("y2", [128, NCC * TH], F32))
        h1g = ee(nc.sbuf_tensor("h1g", [128, 8 * TH], BF16))
        upf = [ee(nc.sbuf_tensor(f"upf{i}", [128, NCC * 128], BF16)) for i in range(2)]
        upb2 = [ee(nc.sbuf_tensor(f"upb{i}", [128, NCC * 128], BF16)) for i in range(2)]
        dnf = [ee(nc.sbuf_tensor(f"dnf{i}", [128, HID], BF16)) for i in range(2)]
        dnb8 = ee(nc.sbuf_tensor("dnb8", [128, 8 * HID], BF16))
        xr = [ee(nc.sbuf_tensor(f"xr{i}", [128, 512], F32)) for i in range(2)]
        rb4 = ee(nc.sbuf_tensor("rb4", [128, 512], BF16))
        rsq4 = ee(nc.sbuf_tensor("rsq4", [128, 512], BF16))
        mrow4 = ee(nc.sbuf_tensor("mrow4", [1, 512], F32))
        vrow4 = ee(nc.sbuf_tensor("vrow4", [1, 512], F32))
        pu = [ee(nc.psum_tensor(f"p34u{i}", [128, 512], F32)) for i in range(2)]
        pd = [ee(nc.psum_tensor(f"p34d{i}", [128, 512], F32)) for i in range(2)]
        ps1 = ee(nc.psum_tensor("p34s1", [1, 512], F32))
        ps2 = ee(nc.psum_tensor("p34s2", [1, 512], F32))
        pr1 = ee(nc.psum_tensor("p34r1", [128, 512], F32))
        pr2 = ee(nc.psum_tensor("p34r2", [128, 512], F32))

        @p.step("tensor", waits=bar, inc="pe4")
        def _(e):
            return e.matmul(pr1[:, 0:1], onesf[:, :], oscal[:, :],
                            start=True, stop=True)
        osr_tok = ("pe4", p.cnt["pe4"])

        @p.step("vector", waits=[osr_tok], inc="dv4")
        def _(e):
            return e.tensor_scalar(out=dseff[:, :], in0=dssc[:, :],
                                   scalar1=pr1[:, 0:1], scalar2=None, op0=ALU.mult)
        dse_tok = ("dv4", p.cnt["dv4"])

        tokp = {}
        for hf in range(2):
            th0 = hf * TH
            for cc in range(NCC):
                @p.step("gpsimd", waits=(bar if (hf == 0 and cc == 0) else
                                         [tokp.get("x1b_free")] if cc == 0 else []),
                        inc="d_x1b", k=16)
                def _(e, cc=cc, th0=th0):
                    return e.dma_start(out=x1b[:, cc * TH:(cc + 1) * TH],
                                       in_=x1f_d[cc * 128:(cc + 1) * 128,
                                                 th0:th0 + TH])
            x1b_tok = p.full("d_x1b")

            @p.step("vector", waits=[tokp.get("y2_free")], inc="dv4")
            def _(e):
                return e.memset(y2[:, :], 0.0)
            y2z_tok = ("dv4", p.cnt["dv4"])

            half_ac = ("ac4", p.cnt["ac4"])
            half_pe = ("pe4", p.cnt["pe4"])
            upl_tok = [None, None]
            upf_free = [half_ac, half_ac]
            dnl_tok = [None, None]
            dnf_free = [half_ac, half_ac]
            dnb_free = half_pe

            def up_load(hc):
                i = hc % 2
                @p.step("gpsimd", waits=[upf_free[i]], inc=f"d_up{i}", k=16)
                def _(e, hc=hc, i=i):
                    return e.dma_start(
                        out=upf[i][:, :].rearrange("p (cc f) -> p cc f", f=128),
                        in_=upT[:, hc * 128:(hc + 1) * 128]
                            .rearrange("(cc p) f -> p cc f", p=128))
                upl_tok[i] = p.full(f"d_up{i}")

            def dn_load(hc):
                i = hc % 2
                @p.step("gpsimd", waits=[dnf_free[i]], inc=f"d_dn{i}", k=16)
                def _(e, hc=hc, i=i):
                    return e.dma_start(out=dnf[i][:, :],
                                       in_=downT[hc * 128:(hc + 1) * 128, :])
                dnl_tok[i] = p.full(f"d_dn{i}")

            up_load(8 * 0)
            upb_free = [half_pe, half_pe]
            h1_free = half_pe
            for g8 in range(8):
                up_toks = []
                for k8 in range(8):
                    hc = g8 * 8 + k8
                    i = hc % 2
                    upb = upb2[hc % 2]
                    @p.step("scalar", waits=[upl_tok[i], upb_free[hc % 2]], inc="ac4")
                    def _(e, i=i, upb=upb):
                        return e.sign(out=upb[:, :], in_=upf[i][:, :])
                    sgn_tok = ("ac4", p.cnt["ac4"])
                    upf_free[i] = sgn_tok
                    if hc + 1 < 64:
                        up_load(hc + 1)

                    mm_toks = []
                    for st2 in range(2):
                        bank = pu[st2]
                        @p.step("tensor", waits=[sgn_tok, x1b_tok,
                                                 tokp.get(("puf", st2))], inc="pe4")
                        def _(e, st2=st2, bank=bank, upb=upb):
                            inst = None
                            for cc in range(NCC):
                                inst = e.matmul(
                                    bank[:, :], upb[:, cc * 128:(cc + 1) * 128],
                                    x1b[:, cc * TH + st2 * 512:cc * TH + (st2 + 1) * 512],
                                    start=(cc == 0), stop=(cc == NCC - 1))
                            return inst
                        umm = ("pe4", p.cnt["pe4"])
                        mm_toks.append(umm)

                        @p.step("scalar", waits=[umm] + (h1_free if isinstance(h1_free, list) else [h1_free]), inc="ac4")
                        def _(e, k8=k8, st2=st2, hc=hc, bank=bank):
                            return e.activation(
                                out=h1g[:, k8 * TH + st2 * 512:k8 * TH + (st2 + 1) * 512],
                                in_=bank[:, :], func=AF.Relu,
                                scale=upsc[:, hc:hc + 1])
                        tokp[("puf", st2)] = ("ac4", p.cnt["ac4"])
                    upb_free[hc % 2] = mm_toks[-1]
                h1_tok = ("ac4", p.cnt["ac4"])

                dn_load(g8 * 8)
                dn_load(g8 * 8 + 1)
                dsg_toks = []
                for k8 in range(8):
                    hc = g8 * 8 + k8
                    i = hc % 2
                    @p.step("scalar", waits=[dnl_tok[i], dnb_free if k8 == 0 else None],
                            inc="ac4")
                    def _(e, i=i, k8=k8):
                        return e.sign(out=dnb8[:, k8 * HID:(k8 + 1) * HID],
                                      in_=dnf[i][:, :])
                    dsg_toks.append(("ac4", p.cnt["ac4"]))
                    dnf_free[i] = dsg_toks[-1]
                    if k8 + 2 < 8:
                        dn_load(g8 * 8 + k8 + 2)
                dsg_tok = dsg_toks[-1]

                if g["dbg3_d"] is not None and hf == 0 and g8 == 0:
                    @p.step("sync", waits=[h1_tok], inc="st", k=16)
                    def _(e):
                        return e.dma_start(
                            out=g["dbg3_d"].rearrange("(k p) t -> p k t", p=128),
                            in_=h1g[:, :].rearrange("p (k t) -> p k t", t=1024)
                                [:, :, 0:512])

                for cc in range(NCC):
                    for st2 in range(2):
                        bank = pd[st2]
                        @p.step("tensor", waits=[dsg_tok, h1_tok,
                                                 tokp.get(("pdf", st2))], inc="pe4")
                        def _(e, cc=cc, st2=st2, bank=bank):
                            inst = None
                            for k8 in range(8):
                                inst = e.matmul(
                                    bank[:, :],
                                    dnb8[:, k8 * HID + cc * 128:k8 * HID + (cc + 1) * 128],
                                    h1g[:, k8 * TH + st2 * 512:k8 * TH + (st2 + 1) * 512],
                                    start=(k8 == 0), stop=(k8 == 7))
                            return inst
                        dmm = ("pe4", p.cnt["pe4"])

                        @p.step("vector", waits=[dmm, y2z_tok], inc="dv4")
                        def _(e, cc=cc, st2=st2, bank=bank):
                            return e.tensor_add(
                                out=y2[:, cc * TH + st2 * 512:cc * TH + (st2 + 1) * 512],
                                in0=y2[:, cc * TH + st2 * 512:cc * TH + (st2 + 1) * 512],
                                in1=bank[:, :])
                        tokp[("pdf", st2)] = ("dv4", p.cnt["dv4"])
                dnb_free = tokp[("pdf", 1)]
                h1_free = tokp[("pdf", 1)]
                if g["dbg3_d"] is not None and hf == 0 and g8 == 0:
                    h1_free = [tokp[("pdf", 1)], p.full("st")]
            y2_tok = ("dv4", p.cnt["dv4"])

            xr_free = [None, None]
            for st2 in range(2):
                ts_glob = hf * 2 + st2
                prev_stat = None
                for cc in range(NCC):
                    i = cc % 2
                    @p.step("scalar", waits=[xr_free[i]], inc=f"d_xs{i}", k=16)
                    def _(e, cc=cc, th0=th0, st2=st2, i=i):
                        return e.dma_start(
                            out=xr[i][:, :],
                            in_=x1f_d[cc * 128:(cc + 1) * 128,
                                      th0 + st2 * 512:th0 + (st2 + 1) * 512])
                    xld = p.full(f"d_xs{i}")

                    @p.step("vector", waits=[xld, y2_tok, dse_tok], inc="dv4")
                    def _(e, cc=cc, st2=st2, i=i):
                        return e.scalar_tensor_tensor(
                            out=y2[:, cc * TH + st2 * 512:cc * TH + (st2 + 1) * 512],
                            in0=y2[:, cc * TH + st2 * 512:cc * TH + (st2 + 1) * 512],
                            scalar=dseff[:, cc:cc + 1], in1=xr[i][:, :],
                            op0=ALU.mult, op1=ALU.add)
                    radd = ("dv4", p.cnt["dv4"])
                    xr_free[i] = radd

                    if g["dbg2_d"] is not None and hf == 0 and st2 == 0:
                        @p.step("sync", waits=[radd], inc="st", k=16)
                        def _(e, cc=cc, st2=st2):
                            return e.dma_start(
                                out=g["dbg2_d"][cc * 128:(cc + 1) * 128, :],
                                in_=y2[:, cc * TH + st2 * 512:cc * TH + (st2 + 1) * 512])

                    @p.step("vector", waits=[prev_stat], inc="dv4")
                    def _(e, cc=cc, st2=st2):
                        return e.tensor_copy(
                            out=rb4[:, :],
                            in_=y2[:, cc * TH + st2 * 512:cc * TH + (st2 + 1) * 512])

                    @p.step("vector", inc="dv4")
                    def _(e):
                        return e.tensor_mul(out=rsq4[:, :], in0=rb4[:, :], in1=rb4[:, :])
                    rsq_tok = ("dv4", p.cnt["dv4"])

                    @p.step("tensor", waits=[rsq_tok], inc="pe4")
                    def _(e, cc=cc):
                        e.matmul(ps1[:, :], onescb[:, :], rb4[:, :],
                                 start=(cc == 0), stop=(cc == NCC - 1))
                        return e.matmul(ps2[:, :], onescb[:, :], rsq4[:, :],
                                        start=(cc == 0), stop=(cc == NCC - 1))
                    prev_stat = ("pe4", p.cnt["pe4"])
                stat_mm = ("pe4", p.cnt["pe4"])

                @p.step("vector", waits=[stat_mm], inc="dv4")
                def _(e):
                    return e.tensor_scalar(out=mrow4[:, :], in0=ps1[0:1, :],
                                           scalar1=1.0 / HID, scalar2=None,
                                           op0=ALU.mult)

                @p.step("vector", inc="dv4")
                def _(e):
                    return e.tensor_scalar(out=vrow4[:, :], in0=ps2[0:1, :],
                                           scalar1=1.0 / HID, scalar2=None,
                                           op0=ALU.mult)

                @p.step("vector", inc="dv4")
                def _(e):
                    return e.tensor_mul(out=rb4[0:1, 0:512], in0=mrow4[:, :],
                                        in1=mrow4[:, :])

                @p.step("vector", inc="dv4")
                def _(e):
                    return e.tensor_sub(out=vrow4[:, :], in0=vrow4[:, :],
                                        in1=rb4[0:1, 0:512])

                @p.step("vector", inc="dv4")
                def _(e):
                    return e.tensor_scalar(out=vrow4[:, :], in0=vrow4[:, :],
                                           scalar1=LN_EPS, scalar2=None, op0=ALU.add)
                var_tok = ("dv4", p.cnt["dv4"])

                @p.step("scalar", waits=[var_tok], inc="ac4")
                def _(e):
                    return e.activation(out=vrow4[:, :], in_=vrow4[:, :], func=AF.Ln)

                @p.step("scalar", inc="ac4")
                def _(e):
                    return e.activation(out=vrow4[:, :], in_=vrow4[:, :], func=AF.Exp,
                                        scale=-0.5)
                rstd_tok = ("ac4", p.cnt["ac4"])

                @p.step("tensor", waits=[rstd_tok, tokp.get("prp4")], inc="pe4")
                def _(e):
                    e.matmul(pr1[:, :], onesf[:, :], mrow4[:, :], start=True, stop=True)
                    return e.matmul(pr2[:, :], onesf[:, :], vrow4[:, :],
                                    start=True, stop=True)
                rep_tok = ("pe4", p.cnt["pe4"])

                for cc in range(NCC):
                    sl4 = lambda cc=cc, st2=st2: y2[:, cc * TH + st2 * 512:
                                                    cc * TH + (st2 + 1) * 512]
                    @p.step("vector", waits=[rep_tok, p.full("st")], inc="dv4")
                    def _(e, cc=cc, st2=st2):
                        s = y2[:, cc * TH + st2 * 512:cc * TH + (st2 + 1) * 512]
                        return e.tensor_sub(out=s, in0=s, in1=pr1[:, :])

                    @p.step("vector", inc="dv4")
                    def _(e, cc=cc, st2=st2):
                        s = y2[:, cc * TH + st2 * 512:cc * TH + (st2 + 1) * 512]
                        return e.tensor_mul(out=s, in0=s, in1=pr2[:, :])
                    odone = ("dv4", p.cnt["dv4"])

                    @p.step("sync", waits=[odone], inc="st", k=16)
                    def _(e, cc=cc, st2=st2, ts_glob=ts_glob):
                        return e.dma_start(
                            out=outT_d[cc * 128:(cc + 1) * 128,
                                       ts_glob * 512:(ts_glob + 1) * 512],
                            in_=y2[:, cc * TH + st2 * 512:cc * TH + (st2 + 1) * 512])
                tokp["prp4"] = ("dv4", p.cnt["dv4"])
            tokp["x1b_free"] = ("pe4", p.cnt["pe4"])
            tokp["y2_free"] = p.full("st")


def host_inputs(inputs, b):
    f32 = np.float32
    x = np.ascontiguousarray(np.asarray(inputs["L0_attn_x"][b], f32).T)
    im = {
        "xT": x,
        "wqkvT": _T(inputs["L0_attn_W_qkv"]),
        "wzT": _T(inputs["L0_attn_W_z"]),
        "wbaT": _T(np.concatenate([np.asarray(inputs["L0_attn_W_b"], f32),
                                   np.asarray(inputs["L0_attn_W_a"], f32)], axis=0)),
        "woutT": _T(inputs["L0_attn_W_out"]),
        "upT": _T(inputs["L0_ffn_up_weight"]),
        "downT": _T(inputs["L0_ffn_down_weight"]),
        "convw": np.ascontiguousarray(np.asarray(inputs["L0_attn_conv_w"], f32)),
        "dtb_c": np.asarray(inputs["L0_attn_dt_bias"], f32).reshape(32, 1),
        "alog_c": np.asarray(inputs["L0_attn_A_log"], f32).reshape(32, 1),
        "normw_c": np.asarray(inputs["L0_attn_norm_w"], f32).reshape(128, 1),
        "upsc_c": np.ascontiguousarray(
            np.asarray(inputs["L0_ffn_up_scale"], f32).reshape(64, 128).T),
        "dssc_c": np.ascontiguousarray(
            np.asarray(inputs["L0_ffn_down_scale"], f32).reshape(16, 128).T),
        "osc_c": np.asarray(inputs["L0_ffn_output_scale"], f32).reshape(1, 1),
        "identb_d": np.eye(128, dtype=BF),
        "identf_d": np.eye(128, dtype=np.float32),
        "maskL_d": np.tril(-np.ones((128, 128), BF), -1),
        "maskUs_d": np.triu(-np.ones((128, 128), BF), 1),
        "maskUi_d": np.triu(np.ones((128, 128), BF), 0),
        "Lz_d": np.concatenate([np.zeros((3, 128), BF), np.ones((3, 128), BF)]),
    }
    return im


def _T(w):
    return np.ascontiguousarray(np.asarray(w, np.float32).T)


_NC_CACHE = {}


def get_nc(phases=(1, 2, 25, 34), dump=False):
    key = (tuple(phases), dump)
    if key not in _NC_CACHE:
        _NC_CACHE[key] = build(phases=phases, dump=dump)
    return _NC_CACHE[key]


def kernel(**inputs):
    nc = get_nc()
    in_maps = [host_inputs(inputs, b) for b in range(8)]
    res = run_bass_kernel_spmd(nc, in_maps, core_ids=list(range(8)))
    out = np.stack([np.ascontiguousarray(r["outT"].T) for r in res.results])
    return out.astype(np.float32)


# version 18 (final)
# speedup vs baseline: 1.0126x; 1.0126x over previous
import numpy as np
import ml_dtypes
from contextlib import ExitStack

import concourse.bass as bass
import concourse.mybir as mybir
from concourse.bass_utils import run_bass_kernel_spmd

F32 = mybir.dt.float32
BF16 = mybir.dt.bfloat16
AF = mybir.ActivationFunctionType
ALU = mybir.AluOpType
BF = ml_dtypes.bfloat16

T = 2048
HID = 2048
QKV = 8192
VD = 4096
NK, DK = 16, 128
NV, DV = 32, 128
P2C = 128
NCH = T // P2C
NCC = HID // 128
NFQ = QKV // 128
NFZ = VD // 128
CL = 60.0
EPS_RMS = 1e-6
LN_EPS = 1e-5

ENGINES = ("tensor", "vector", "scalar", "gpsimd", "sync")


class Prog:

    def __init__(self):
        self.streams = {e: [] for e in ENGINES}
        self.cnt = {}
        self.waited = {e: {} for e in ENGINES}
        self.sem = {}
        self.last = {e: None for e in ENGINES}
        self.dma_sems = set()

    def new_sems(self, *names):
        for n in names:
            self.cnt[n] = 0

    def bump(self, name, k=1):
        self.cnt[name] = self.cnt.get(name, 0) + k
        return (name, self.cnt[name])

    def full(self, name):
        return (name, self.cnt.get(name, 0))

    def step(self, eng, waits=(), inc=None, k=1):
        waits = list(waits)
        if eng in ("vector", "scalar", "gpsimd") and self.last[eng] is not None:
            waits.append(self.last[eng])
        tok = self.bump(inc, k) if inc else None
        if inc is not None:
            self.last[eng] = tok
            if k == 16:
                self.dma_sems.add(inc)
        wl = []
        for w in waits:
            if w is None:
                continue
            s, v = w
            if v > self.waited[eng].get(s, 0):
                self.waited[eng][s] = v
                wl.append((s, v))

        def reg(fn):
            def emit(e):
                for (s, v) in wl:
                    e.wait_ge(self.sem[s], v)
                inst = fn(e)
                if inc is not None:
                    assert inst is not None, "step with inc must return last inst"
                    inst.then_inc(self.sem[inc], k)
            self.streams[eng].append(emit)
            return tok
        return reg


def ceil_div(a, b):
    return (a + b - 1) // b


def build(phases=(1, 2, 25, 34), dump=False):
    nc = bass.Bass("TRN2", target_bir_lowering=False)
    p = Prog()
    sems_needed = []
    for ph in ("0", "1", "2", "3", "4"):
        for s in ("pe", "dv", "ac", "po", "qs", "qg"):
            sems_needed.append(s + ph)
    sems_needed.append("st")
    p.new_sems(*sems_needed)

    def din(name, shape, dtype=F32):
        return nc.dram_tensor(name, shape, dtype, kind="ExternalInput").ap()

    xT = din("xT", [HID, T])
    wqkvT = din("wqkvT", [HID, QKV])
    wzT = din("wzT", [HID, VD])
    wbaT = din("wbaT", [HID, 64])
    woutT = din("woutT", [VD, HID])
    upT = din("upT", [HID, QKV])
    downT = din("downT", [QKV, HID])
    convw = din("convw", [QKV, 4])
    dtb_c = din("dtb_c", [32, 1])
    alog_c = din("alog_c", [32, 1])
    normw_c = din("normw_c", [128, 1])
    upsc_c = din("upsc_c", [128, 64])
    dssc_c = din("dssc_c", [128, 16])
    osc_c = din("osc_c", [1, 1])
    identb_d = din("identb_d", [128, 128], BF16)
    identf_d = din("identf_d", [128, 128], F32)
    maskL_d = din("maskL_d", [128, 128], BF16)
    maskUs_d = din("maskUs_d", [128, 128], BF16)
    maskUi_d = din("maskUi_d", [128, 128], BF16)
    Lz_d = din("Lz_d", [6, 128], BF16)

    outT = nc.dram_tensor("outT", [HID, T], F32, kind="ExternalOutput").ap()

    skind = "ExternalOutput" if dump else "Internal"
    qkvc_d = nc.dram_tensor("qkvc_s", [QKV, T], BF16, kind=skind).ap()
    zsil_d = nc.dram_tensor("zsil_s", [VD, T], BF16, kind=skind).ap()
    og_d = nc.dram_tensor("og_s", [VD, T], BF16, kind=skind).ap()
    x1f_d = nc.dram_tensor("x1f_s", [HID, T], F32, kind=skind).ap()
    dbg_d = nc.dram_tensor("dbg_s", [64, T], F32, kind=skind).ap() if dump else None
    dbg2_d = nc.dram_tensor("dbg2_s", [HID, 512], F32, kind=skind).ap() if dump else None
    dbg4_d = nc.dram_tensor("dbg4_s", [64, T], F32, kind=skind).ap() if dump else None
    dbg5_d = nc.dram_tensor("dbg5_s", [64, T], F32, kind=skind).ap() if dump else None
    dbg3_d = nc.dram_tensor("dbg3_s", [1024, 512], BF16, kind=skind).ap() if dump else None

    ctx = ExitStack()
    with ctx:
        e_ = ctx.enter_context

        identb = e_(nc.sbuf_tensor("identb", [128, 128], BF16))
        identf = e_(nc.sbuf_tensor("identf", [128, 128], F32))
        maskL = e_(nc.sbuf_tensor("maskL", [128, 128], BF16))
        maskUs = e_(nc.sbuf_tensor("maskUs", [128, 128], BF16))
        maskUi = e_(nc.sbuf_tensor("maskUi", [128, 128], BF16))
        onesb = e_(nc.sbuf_tensor("onesb", [1, 128], BF16))
        onescb = e_(nc.sbuf_tensor("onescb", [128, 1], BF16))
        normw = e_(nc.sbuf_tensor("normw", [128, 1], F32))
        dtb = e_(nc.sbuf_tensor("dtb", [32, 1], F32))
        negexpA = e_(nc.sbuf_tensor("negexpA", [32, 1], F32))
        upsc = e_(nc.sbuf_tensor("upsc", [128, 64], F32))
        dssc = e_(nc.sbuf_tensor("dssc", [128, 16], F32))
        dseff = e_(nc.sbuf_tensor("dseff", [128, 16], F32))
        oscal = e_(nc.sbuf_tensor("oscal", [1, 1], F32))
        onesf = e_(nc.sbuf_tensor("onesf", [1, 128], F32))
        beta_b = e_(nc.sbuf_tensor("beta_b", [32, T], BF16))
        lnbeta = e_(nc.sbuf_tensor("lnbeta", [32, T], F32))
        cpad = e_(nc.sbuf_tensor("cpad", [32, T + 128], F32))

        ld0 = []
        for (dst, src) in [(identb, identb_d), (identf, identf_d), (maskL, maskL_d),
                           (maskUs, maskUs_d), (maskUi, maskUi_d), (normw, normw_c),
                           (dtb, dtb_c), (upsc, upsc_c), (dssc, dssc_c), (osc := oscal, osc_c),
                           (negexpA, alog_c)]:
            @p.step("sync", inc="c0", k=16)
            def _(e, dst=dst, src=src):
                return e.dma_start(out=dst[:, :], in_=src)
        ld0_tok = p.full("c0")
        alog_tok = ld0_tok

        @p.step("vector", waits=[ld0_tok], inc="dv0")
        def _(e):
            return e.memset(onesb[:, :], 1.0)

        @p.step("vector", inc="dv0")
        def _(e):
            return e.memset(onescb[:, :], 1.0)

        @p.step("vector", inc="dv0")
        def _(e):
            return e.memset(cpad[:, 0:128], 0.0)

        @p.step("vector", inc="dv0")
        def _(e):
            return e.memset(onesf[:, :], 1.0)
        memset_tok = ("dv0", p.cnt["dv0"])

        @p.step("scalar", waits=[alog_tok], inc="ac0")
        def _(e):
            return e.activation(out=negexpA[:, :], in_=negexpA[:, :], func=AF.Exp)
        @p.step("vector", waits=[("ac0", p.cnt["ac0"])], inc="dv0")
        def _(e):
            return e.tensor_scalar(out=negexpA[:, :], in0=negexpA[:, :], scalar1=-1.0,
                                   scalar2=None, op0=ALU.mult)
        negexpA_tok = ("dv0", p.cnt["dv0"])

        env = dict(locals())
        if 1 in phases:
            emit_p1(nc, p, ctx, env)
        if 2 in phases:
            emit_p2(nc, p, ctx, env)
        if 25 in phases:
            emit_p25(nc, p, ctx, env)
        if 34 in phases:
            emit_p34(nc, p, ctx, env)

        @p.step("sync", waits=[("st", p.cnt["st"])] +
                [(s, p.cnt[s]) for s in sorted(p.dma_sems)])
        def _(e):
            return None

        with ExitStack() as sctx:
            for name in p.cnt:
                p.sem[name] = sctx.enter_context(nc.semaphore(name))
            blk = sctx.enter_context(nc.Block())

            @blk.tensor
            def _(eng):
                for f in p.streams["tensor"]:
                    f(eng)

            @blk.vector
            def _(eng):
                for f in p.streams["vector"]:
                    f(eng)

            @blk.scalar
            def _(eng):
                for f in p.streams["scalar"]:
                    f(eng)

            @blk.gpsimd
            def _(eng):
                for f in p.streams["gpsimd"]:
                    f(eng)

            @blk.sync
            def _(eng):
                for f in p.streams["sync"]:
                    f(eng)

    return nc


def emit_p1(nc, p, ctx, env):
    e_ = ctx.enter_context
    g = env
    xT_d, wqkvT, wzT, wbaT, convw = g["xT"], g["wqkvT"], g["wzT"], g["wbaT"], g["convw"]
    qkvc_d, zsil_d = g["qkvc_d"], g["zsil_d"]
    beta_b, lnbeta, cpad = g["beta_b"], g["lnbeta"], g["cpad"]
    dtb, negexpA = g["dtb"], g["negexpA"]
    memset_tok = g["memset_tok"]
    negexpA_tok = g["negexpA_tok"]

    p1 = ExitStack()
    with p1:
        ee = p1.enter_context
        xTb = ee(nc.sbuf_tensor("xTb", [128, NCC * T], BF16))
        wbuf = [ee(nc.sbuf_tensor(f"wbuf{i}", [128, NCC * 128], BF16)) for i in range(2)]
        wf32 = [ee(nc.sbuf_tensor(f"wf32{i}", [128, NCC * 128], F32)) for i in range(2)]
        qrow = [ee(nc.sbuf_tensor(f"qrow{i}", [128, 3 + T], F32)) for i in range(2)]
        acc1 = ee(nc.sbuf_tensor("acc1", [128, T], F32))
        acc2 = ee(nc.sbuf_tensor("acc2", [128, T], F32))
        acc4 = ee(nc.sbuf_tensor("acc4", [128, T], F32))
        acc5 = ee(nc.sbuf_tensor("acc5", [128, T], F32))
        obuf = [ee(nc.sbuf_tensor(f"obuf{i}", [128, T], BF16)) for i in range(2)]
        cw = ee(nc.sbuf_tensor("cw", [128, NFQ * 4], F32))
        wba = ee(nc.sbuf_tensor("wba", [128, NCC * 64], BF16))
        bdtraw = ee(nc.sbuf_tensor("bdtraw", [64, T], F32))
        t32a = ee(nc.sbuf_tensor("t32a", [32, T], F32))
        t32b = ee(nc.sbuf_tensor("t32b", [32, T], F32))
        zeros32 = ee(nc.sbuf_tensor("zeros32", [32, T], F32))
        pq = [ee(nc.psum_tensor(f"pq{i}", [128, 512], F32)) for i in range(4)]
        pba = ee(nc.psum_tensor("pba", [64, 512], F32))

        for cc in range(NCC):
            @p.step("gpsimd", inc="d_x", k=16)
            def _(e, cc=cc):
                return e.dma_start(out=xTb[:, cc * T:(cc + 1) * T],
                                   in_=xT_d[cc * 128:(cc + 1) * 128, :])
        x_tok = p.full("d_x")

        @p.step("vector", inc="dv1")
        def _(e):
            return e.memset(qrow[0][:, 0:3], 0.0)

        @p.step("vector", inc="dv1")
        def _(e):
            return e.memset(qrow[1][:, 0:3], 0.0)
        qpad_tok = ("dv1", p.cnt["dv1"])

        @p.step("sync", inc="d_cw", k=16)
        def _(e):
            return e.dma_start(
                out=cw[:, :].rearrange("p (fc j) -> p fc j", j=4),
                in_=convw.rearrange("(fc p) j -> p fc j", p=128))
        cw_tok = p.full("d_cw")

        @p.step("gpsimd", inc="d_wba", k=16)
        def _(e):
            return e.dma_start(
                out=wba[:, :].rearrange("p (cc f) -> p cc f", f=64),
                in_=wbaT.rearrange("(cc p) f -> p cc f", p=128))
        wba_tok = p.full("d_wba")

        sched = [("qkv", fc) for fc in range(NFQ)] + [("z", fc) for fc in range(NFZ)]

        wload_tok = [None, None]
        wfree_tok = [None, None]
        qrow_free = [None, None]
        obuf_free = [None, None]
        accA_free = None
        acc3_free = None
        zfree = [None, None]
        acc2_free = None
        pq_free = [None, None, None, None]

        wcast_tok = [None, None]

        def wload(i, kind, fc):
            src = wqkvT if kind == "qkv" else wzT
            @p.step("sync", waits=[wcast_tok[i]], inc=f"d_w{i}", k=16)
            def _(e, src=src, fc=fc, i=i):
                return e.dma_start(
                    out=wf32[i][:, :].rearrange("p (cc f) -> p cc f", f=128),
                    in_=src[:, fc * 128:(fc + 1) * 128].rearrange("(cc p) f -> p cc f", p=128))
            wl = p.full(f"d_w{i}")

            @p.step("vector", waits=[wl, wfree_tok[i]], inc="dv1")
            def _(e, i=i):
                return e.tensor_copy(out=wbuf[i][:, :], in_=wf32[i][:, :])
            wload_tok[i] = ("dv1", p.cnt["dv1"])
            wcast_tok[i] = wload_tok[i]

        wload(0, *sched[0])
        wload(1, *sched[1])

        for si, (kind, fc) in enumerate(sched):
            par = si % 2
            @p.step("tensor", waits=[wload_tok[par], x_tok, pq_free[0], pq_free[1],
                                     pq_free[2], pq_free[3]], inc="pe1")
            def _(e, par=par):
                inst = None
                for cc in range(NCC):
                    for ts in range(4):
                        inst = e.matmul(pq[ts][:, :],
                                        wbuf[par][:, cc * 128:(cc + 1) * 128],
                                        xTb[:, cc * T + ts * 512: cc * T + (ts + 1) * 512],
                                        start=(cc == 0), stop=(cc == NCC - 1))
                return inst
            mm_tok = ("pe1", p.cnt["pe1"])
            mm_toks = [mm_tok] * 4

            evac_toks = []
            for ts in range(4):
                @p.step("scalar", waits=[mm_tok, qrow_free[par], qpad_tok], inc="ac1")
                def _(e, par=par, ts=ts):
                    return e.copy(out=qrow[par][:, 3 + ts * 512: 3 + (ts + 1) * 512],
                                  in_=pq[ts][:, :])
                evac_toks.append(("ac1", p.cnt["ac1"]))
                pq_free[ts] = evac_toks[-1]
            qrow_tok = evac_toks[-1]

            if kind == "qkv":
                @p.step("vector", waits=[qrow_tok, cw_tok, accA_free], inc="dv1")
                def _(e, par=par, fc=fc):
                    return e.tensor_scalar(out=acc1[:, :], in0=qrow[par][:, 3:3 + T],
                                           scalar1=cw[:, fc * 4 + 3: fc * 4 + 4],
                                           scalar2=None, op0=ALU.mult)

                @p.step("vector", inc="dv1")
                def _(e, par=par, fc=fc):
                    return e.scalar_tensor_tensor(out=acc2[:, :], in0=qrow[par][:, 2:2 + T],
                                                  scalar=cw[:, fc * 4 + 2: fc * 4 + 3],
                                                  in1=acc1[:, :], op0=ALU.mult, op1=ALU.add)

                @p.step("vector", inc="dv1")
                def _(e, par=par, fc=fc):
                    return e.scalar_tensor_tensor(out=acc1[:, :], in0=qrow[par][:, 1:1 + T],
                                                  scalar=cw[:, fc * 4 + 1: fc * 4 + 2],
                                                  in1=acc2[:, :], op0=ALU.mult, op1=ALU.add)
                @p.step("vector", waits=[obuf_free[par]], inc="dv1")
                def _(e, par=par, fc=fc):
                    return e.scalar_tensor_tensor(out=obuf[par][:, :],
                                                  in0=qrow[par][:, 0:T],
                                                  scalar=cw[:, fc * 4: fc * 4 + 1],
                                                  in1=acc1[:, :],
                                                  op0=ALU.mult, op1=ALU.add)
                pool_tok = ("dv1", p.cnt["dv1"])
                qrow_free[par] = pool_tok
                accA_free = pool_tok

                @p.step("sync", waits=[pool_tok], inc=f"d_ob{par}", k=16)
                def _(e, par=par, fc=fc):
                    return e.dma_start(out=qkvc_d[fc * 128:(fc + 1) * 128, :],
                                       in_=obuf[par][:, :])
                obuf_free[par] = p.full(f"d_ob{par}")
            else:
                if zfree[0] is None and zfree[1] is None:
                    zfree = [accA_free, accA_free]
                za = [acc1, acc4][par]
                zb = [acc2, acc5][par]
                @p.step("scalar", waits=[qrow_tok, zfree[par]], inc="ac1")
                def _(e, par=par, za=za):
                    return e.activation(out=za[:, :], in_=qrow[par][:, 3:3 + T],
                                        func=AF.Exp, scale=-1.0)
                exp_tok = ("ac1", p.cnt["ac1"])

                @p.step("vector", waits=[exp_tok], inc="dv1")
                def _(e, za=za, zb=zb):
                    return e.tensor_scalar(out=zb[:, :], in0=za[:, :], scalar1=1.0,
                                           scalar2=None, op0=ALU.add)

                @p.step("vector", inc="dv1")
                def _(e, za=za, zb=zb):
                    return e.reciprocal(out=za[:, :], in_=zb[:, :])

                @p.step("vector", waits=[obuf_free[par]], inc="dv1")
                def _(e, par=par, za=za):
                    return e.tensor_mul(out=obuf[par][:, :], in0=qrow[par][:, 3:3 + T],
                                        in1=za[:, :])
                sil_tok = ("dv1", p.cnt["dv1"])
                qrow_free[par] = sil_tok
                zfree[par] = sil_tok
                accA_free = sil_tok if par == 0 else accA_free

                @p.step("sync", waits=[sil_tok], inc=f"d_ob{par}", k=16)
                def _(e, par=par, fc=fc):
                    return e.dma_start(out=zsil_d[fc * 128:(fc + 1) * 128, :],
                                       in_=obuf[par][:, :])
                obuf_free[par] = p.full(f"d_ob{par}")

            wfree_tok[par] = mm_toks[-1]
            if si + 2 < len(sched):
                wload(par, *sched[si + 2])

        ba_mm = []
        pba_free = None
        for ts in range(4):
            @p.step("tensor", waits=[wba_tok, x_tok, pba_free], inc="pe1")
            def _(e, ts=ts):
                inst = None
                for cc in range(NCC):
                    inst = e.matmul(pba[:, :],
                                    wba[:, cc * 64:(cc + 1) * 64],
                                    xTb[:, cc * T + ts * 512: cc * T + (ts + 1) * 512],
                                    start=(cc == 0), stop=(cc == NCC - 1))
                return inst
            mm_tok = ("pe1", p.cnt["pe1"])

            @p.step("scalar", waits=[mm_tok], inc="ac1")
            def _(e, ts=ts):
                return e.copy(out=bdtraw[:, ts * 512:(ts + 1) * 512], in_=pba[:, :])
            ba_mm.append(("ac1", p.cnt["ac1"]))
            pba_free = ba_mm[-1]
        braw_tok = ba_mm[-1]

        @p.step("vector", waits=[braw_tok, memset_tok], inc="dv1")
        def _(e):
            return e.tensor_scalar(out=t32a[:, :], in0=bdtraw[32:64, :],
                                   scalar1=dtb[:, :], scalar2=None, op0=ALU.add)
        xp_tok = ("dv1", p.cnt["dv1"])

        @p.step("scalar", waits=[xp_tok], inc="ac1")
        def _(e):
            return e.activation(out=t32b[:, :], in_=t32a[:, :], func=AF.Abs)

        @p.step("scalar", inc="ac1")
        def _(e):
            return e.activation(out=t32b[:, :], in_=t32b[:, :], func=AF.Exp, scale=-1.0)

        @p.step("scalar", inc="ac1")
        def _(e):
            return e.activation(out=t32b[:, :], in_=t32b[:, :], func=AF.Ln, bias=1.0)
        l_tok = ("ac1", p.cnt["ac1"])

        @p.step("vector", waits=[l_tok, negexpA_tok], inc="dv1")
        def _(e):
            return e.tensor_scalar(out=t32a[:, :], in0=t32a[:, :], scalar1=0.0,
                                   scalar2=None, op0=ALU.max)

        @p.step("vector", inc="dv1")
        def _(e):
            return e.tensor_add(out=t32a[:, :], in0=t32a[:, :], in1=t32b[:, :])

        @p.step("vector", inc="dv1")
        def _(e):
            return e.tensor_scalar(out=t32a[:, :], in0=t32a[:, :],
                                   scalar1=negexpA[:, :], scalar2=None, op0=ALU.mult)

        @p.step("vector", inc="dv1")
        def _(e):
            return e.memset(zeros32[:, :], 0.0)
        gval_tok = ("dv1", p.cnt["dv1"])

        if g["dbg_d"] is not None:
            @p.step("sync", waits=[gval_tok], inc="st", k=16)
            def _(e):
                return e.dma_start(out=g["dbg5_d"][0:32, :], in_=t32a[:, :])

            @p.step("sync", inc="st", k=16)
            def _(e):
                return e.dma_start(out=g["dbg5_d"][32:64, :], in_=t32b[:, :])
        dump5 = p.full("st")

        @p.step("vector", waits=[dump5], inc="dv1")
        def _(e):
            return e.tensor_tensor_scan(out=cpad[:, 128:], data0=t32a[:, :],
                                        data1=zeros32[:, :], initial=0.0,
                                        op0=ALU.add, op1=ALU.add)
        g["cpad_tok"] = ("dv1", p.cnt["dv1"])

        @p.step("scalar", waits=[braw_tok, g["cpad_tok"]], inc="ac1")
        def _(e):
            return e.activation(out=t32b[:, :], in_=bdtraw[0:32, :], func=AF.Abs)

        @p.step("scalar", inc="ac1")
        def _(e):
            return e.activation(out=t32b[:, :], in_=t32b[:, :], func=AF.Exp, scale=-1.0)

        @p.step("scalar", inc="ac1")
        def _(e):
            return e.activation(out=t32b[:, :], in_=t32b[:, :], func=AF.Ln, bias=1.0)
        lb_tok = ("ac1", p.cnt["ac1"])

        @p.step("vector", waits=[lb_tok], inc="dv1")
        def _(e):
            return e.tensor_scalar(out=lnbeta[:, :], in0=bdtraw[0:32, :], scalar1=0.0,
                                   scalar2=None, op0=ALU.min)

        @p.step("vector", inc="dv1")
        def _(e):
            return e.tensor_sub(out=lnbeta[:, :], in0=lnbeta[:, :], in1=t32b[:, :])
        g["lnbeta_tok"] = ("dv1", p.cnt["dv1"])

        @p.step("scalar", waits=[braw_tok], inc="ac1")
        def _(e):
            return e.activation(out=t32a[:, :], in_=bdtraw[0:32, :], func=AF.Exp,
                                scale=-1.0)
        e2_tok = ("ac1", p.cnt["ac1"])
        @p.step("vector", waits=[e2_tok], inc="dv1")
        def _(e):
            return e.tensor_scalar(out=t32a[:, :], in0=t32a[:, :], scalar1=1.0,
                                   scalar2=None, op0=ALU.add)

        @p.step("vector", inc="dv1")
        def _(e):
            return e.reciprocal(out=t32b[:, :], in_=t32a[:, :])

        @p.step("vector", inc="dv1")
        def _(e):
            return e.tensor_copy(out=beta_b[:, :], in_=t32b[:, :])
        g["beta_tok"] = ("dv1", p.cnt["dv1"])

        if g["dbg_d"] is not None:
            @p.step("sync", waits=[g["beta_tok"], g["cpad_tok"], g["lnbeta_tok"]],
                    inc="st", k=16)
            def _(e):
                return e.dma_start(out=g["dbg_d"][0:32, :], in_=cpad[:, 128:])

            @p.step("sync", inc="st", k=16)
            def _(e):
                return e.dma_start(out=g["dbg_d"][32:64, :], in_=lnbeta[:, :])

            @p.step("sync", inc="st", k=16)
            def _(e):
                return e.dma_start(out=g["dbg4_d"], in_=bdtraw[:, :])


def emit_p2(nc, p, ctx, env):
    g = env
    qkvc_d, zsil_d, og_d = g["qkvc_d"], g["zsil_d"], g["og_d"]
    beta_b, lnbeta, cpad = g["beta_b"], g["lnbeta"], g["cpad"]
    identb, identf = g["identb"], g["identf"]
    maskL, maskUs, maskUi = g["maskL"], g["maskUs"], g["maskUi"]
    onesb, onescb, normw = g["onesb"], g["onescb"], g["normw"]
    onesf = g["onesf"]
    Lz_d = g["Lz_d"]

    bar = [("pe1", p.cnt["pe1"]), ("dv1", p.cnt["dv1"]), ("ac1", p.cnt["ac1"]),
           ("po1", p.cnt["po1"]), p.full("d_ob0"), p.full("d_ob1"), p.full("st")]

    p2 = ExitStack()
    with p2:
        ee = p2.enter_context
        qf = [ee(nc.sbuf_tensor(f"qf{i}", [128, NK * 128], BF16)) for i in range(2)]
        kf1 = ee(nc.sbuf_tensor("kf1", [128, NK * 128], BF16))
        kf = [kf1, kf1]
        ktm2 = [ee(nc.sbuf_tensor(f"ktm{i}", [128, NK * 128], BF16)) for i in range(2)]
        qtm = ee(nc.sbuf_tensor("qtm", [128, NK * 128], BF16))
        vtm = ee(nc.sbuf_tensor("vtm", [128, NV * 128], BF16))
        zf = ee(nc.sbuf_tensor("zf", [128, NV * 128], BF16))
        Sb = ee(nc.sbuf_tensor("Sb", [128, NK * 256], BF16))
        Gl = ee(nc.sbuf_tensor("Gl", [128, NK * 128], BF16))
        Gu = ee(nc.sbuf_tensor("Gu", [128, NK * 128], BF16))
        Qu = ee(nc.sbuf_tensor("Qu", [128, NK * 128], BF16))
        Aw = ee(nc.sbuf_tensor("Aw", [128, NV * 128], BF16))
        ATw = ee(nc.sbuf_tensor("ATw", [128, NV * 128], BF16))
        Wow = ee(nc.sbuf_tensor("Wow", [128, NV * 128], BF16))
        R3w = ee(nc.sbuf_tensor("R3w", [128, NV * 128], BF16))
        LU1 = ee(nc.sbuf_tensor("LU1", [6, NV * 128], BF16))
        LU2 = LU1
        RU1 = ee(nc.sbuf_tensor("RU1", [6, NV * 128], BF16))
        RU2 = ee(nc.sbuf_tensor("RU2", [6, NV * 128], BF16))
        RU3 = ee(nc.sbuf_tensor("RU3", [6, NV * 128], BF16))
        Lz = ee(nc.sbuf_tensor("Lz", [6, 128], BF16))
        Aw2 = ee(nc.sbuf_tensor("Aw2", [128, NV * 128], BF16))
        ATw2 = ee(nc.sbuf_tensor("ATw2", [128, NV * 128], BF16))
        xkbw = ee(nc.sbuf_tensor("xkbw", [128, NV * 128], BF16))
        ubv = ee(nc.sbuf_tensor("ubv", [128, 512], F32))
        Lqsc = [ee(nc.sbuf_tensor(f"Lqsc{L}", [128, 512], BF16)) for L in range(2)]
        Lobf = [ee(nc.sbuf_tensor(f"Lobf{L}", [128, 512], BF16)) for L in range(2)]
        Lkde = [ee(nc.sbuf_tensor(f"Lkde{L}", [128, 512], BF16)) for L in range(2)]
        Lrof = [ee(nc.sbuf_tensor(f"Lrof{L}", [1, 512], F32)) for L in range(2)]
        Lrob = [ee(nc.sbuf_tensor(f"Lrob{L}", [1, 512], BF16)) for L in range(2)]
        Usp = Lqsc[0]
        Utr = xkbw

        Ur1 = ee(nc.sbuf_tensor("Ur1", [128, 32], F32))
        Ur2 = ee(nc.sbuf_tensor("Ur2", [128, 32], F32))
        Etmp2 = [ee(nc.sbuf_tensor(f"Etmp{i}", [128, 512], F32)) for i in range(2)]
        trash = ee(nc.sbuf_tensor("trash", [128, 128], BF16))
        cl60 = ee(nc.sbuf_tensor("cl60", [128, 1], F32))
        ogb1 = ee(nc.sbuf_tensor("ogb1", [128, NV * 128], BF16))
        ogb = [ogb1, ogb1]
        sm = {}
        for n in ("cTt", "cpTt", "lnbT", "bTf", "U1", "U2n", "U3", "U3s",
                  "negeU1s", "Edec", "Sdec", "Cp0rep", "CCrep", "targ"):
            sm[n] = ee(nc.sbuf_tensor("sm_" + n, [128, 32], F32))
        bTb = ee(nc.sbuf_tensor("bTb", [128, 32], BF16))
        nk2 = ee(nc.sbuf_tensor("nk2", [128, 16], F32))
        nq2 = ee(nc.sbuf_tensor("nq2", [128, 16], F32))
        lnk = ee(nc.sbuf_tensor("lnk", [128, 16], F32))
        lnq = ee(nc.sbuf_tensor("lnq", [128, 16], F32))
        rA = ee(nc.sbuf_tensor("rA", [1, 64], F32))
        ptr_b = ee(nc.psum_tensor("ptr_b", [128, 1024], BF16))
        ptr_f = ee(nc.psum_tensor("ptr_f", [128, 512], F32))
        pe_a = ee(nc.psum_tensor("pe_a", [128, 512], F32))
        pe_b = ee(nc.psum_tensor("pe_b", [128, 512], F32))
        ps_a = ee(nc.psum_tensor("ps_a", [128, 512], F32))
        ps_b = ee(nc.psum_tensor("ps_b", [128, 512], F32))
        px = ee(nc.psum_tensor("px", [128, 512], F32))
        py = ee(nc.psum_tensor("py", [128, 512], F32))

        @p.step("sync", waits=bar, inc="d_lz", k=16)
        def _(e):
            return e.dma_start(out=Lz[:, :], in_=Lz_d)
        lz_tok = p.full("d_lz")

        init_toks = []
        first = True
        for stg in (LU1, RU1, RU2, RU3):
            @p.step("vector", waits=bar if first else [], inc="dv2")
            def _(e, stg=stg):
                return e.memset(stg[:, :], 1.0)
            first = False
        @p.step("vector", inc="dv2")
        def _(e):
            return e.memset(Sb[:, :], 0.0)

        @p.step("vector", inc="dv2")
        def _(e):
            inst = None
            for ci in range(4):
                inst = e.memset(Usp[:, ci * 128 + 96:(ci + 1) * 128], 0.0)
            return inst

        @p.step("vector", inc="dv2")
        def _(e):
            return e.memset(cl60[:, :], CL)
        init_tok = ("dv2", p.cnt["dv2"])

        tok = {
            "qf": [None, None], "kf": [None, None],
            "ktm": None, "qtm": None, "vtm": None, "zf": None,
            "stg": init_tok,
            "ogb": [None, None],
            "Sb_w": init_tok,
        }
        ld = {}

        def chunk_loads(ch, firstwaits=()):
            par = ch % 2
            t0 = ch * P2C
            @p.step("gpsimd", waits=list(firstwaits) + [tok["qf"][par]],
                    inc=f"d_qf{par}", k=16)
            def _(e):
                return e.dma_start(
                    out=qf[par][:, :].rearrange("p (h t) -> p h t", t=128),
                    in_=qkvc_d[0:QD2, t0:t0 + 128].rearrange("(h p) t -> p h t", p=128))

            @p.step("gpsimd", waits=list(firstwaits) + [tok["kf"][par]],
                    inc=f"d_kf{par}", k=16)
            def _(e):
                return e.dma_start(
                    out=kf[par][:, :].rearrange("p (h t) -> p h t", t=128),
                    in_=qkvc_d[QD2:2 * QD2, t0:t0 + 128].rearrange("(h p) t -> p h t", p=128))
            ld[("qf", ch)] = p.full(f"d_qf{par}")
            ld[("kf", ch)] = p.full(f"d_kf{par}")
            @p.step("sync", waits=list(firstwaits) + [tok["ktm"]], inc="d_ktm", k=16)
            def _(e, par=par):
                return e.dma_start(out=ktm2[par][:, :],
                                   in_=qkvc_d[QD2:2 * QD2, t0:t0 + 128],
                                   transpose=True)

            @p.step("sync", waits=[tok["qtm"]], inc="d_qtm", k=16)
            def _(e):
                return e.dma_start(out=qtm[:, :], in_=qkvc_d[0:QD2, t0:t0 + 128],
                                   transpose=True)

            @p.step("sync", waits=[tok["vtm"]], inc="d_vtm", k=16)
            def _(e):
                return e.dma_start(out=vtm[:, :], in_=qkvc_d[2 * QD2:QKV, t0:t0 + 128],
                                   transpose=True)
            ld[("ktm", ch)] = p.full("d_ktm")
            ld[("qtm", ch)] = p.full("d_qtm")
            ld[("vtm", ch)] = p.full("d_vtm")

            @p.step("scalar", waits=list(firstwaits) + [tok["zf"]], inc="d_zf", k=16)
            def _(e):
                return e.dma_start(
                    out=zf[:, :].rearrange("p (h t) -> p h t", t=128),
                    in_=zsil_d[:, t0:t0 + 128].rearrange("(h p) t -> p h t", p=128))
            ld[("zf", ch)] = p.full("d_zf")

        QD2 = 2048
        chunk_loads(0, firstwaits=bar + [init_tok])

        for ch in range(NCH):
            par = ch % 2
            t0 = ch * P2C
            co = 128 + t0

            @p.step("tensor",
                    waits=(bar + [init_tok]) if ch == 0 else [tok.get("ptrf_free")],
                    inc="pe2")
            def _(e, co=co, t0=t0):
                i32 = identf[0:32, 0:32]
                e.transpose(ptr_f[:, 0:32], cpad[:, co:co + 128], i32)
                e.transpose(ptr_f[:, 32:64], cpad[:, co - 1:co + 127], i32)
                e.transpose(ptr_f[:, 64:96], lnbeta[:, t0:t0 + 128], i32)
                e.transpose(ptr_f[0:1, 96:128], cpad[:, co - 1:co], i32)
                return e.transpose(ptr_f[0:1, 128:160], cpad[:, co + 127:co + 128], i32)
            trf_tok = ("pe2", p.cnt["pe2"])

            @p.step("tensor", waits=[tok.get("ptrb_free")], inc="pe2")
            def _(e, t0=t0):
                return e.transpose(ptr_b[:, 0:32], beta_b[:, t0:t0 + 128],
                                   identb[0:32, 0:32])
            trb_tok = ("pe2", p.cnt["pe2"])

            @p.step("vector", waits=[trf_tok], inc="dv2")
            def _(e):
                return e.tensor_copy(out=sm["cTt"][:, :], in_=ptr_f[:, 0:32])

            @p.step("vector", inc="dv2")
            def _(e):
                return e.tensor_copy(out=sm["cpTt"][:, :], in_=ptr_f[:, 32:64])

            @p.step("vector", inc="dv2")
            def _(e):
                return e.tensor_copy(out=sm["lnbT"][:, :], in_=ptr_f[:, 64:96])

            @p.step("vector", inc="dv2")
            def _(e):
                e.tensor_copy(out=rA[:, 0:32], in_=ptr_f[0:1, 96:128])
                return e.tensor_copy(out=rA[:, 32:64], in_=ptr_f[0:1, 128:160])

            @p.step("vector", waits=[trb_tok], inc="dv2")
            def _(e):
                return e.tensor_copy(out=sm["bTf"][:, :], in_=ptr_b[:, 0:32])

            @p.step("vector", inc="dv2")
            def _(e):
                return e.tensor_copy(out=bTb[:, :], in_=ptr_b[:, 0:32])
            prep_evac_tok = ("dv2", p.cnt["dv2"])
            tok["ptrf_free0"] = prep_evac_tok

            @p.step("tensor", waits=[prep_evac_tok], inc="pe2")
            def _(e):
                e.matmul(ptr_f[:, 160:192], onesf[:, :], rA[:, 0:32],
                         start=True, stop=True)
                return e.matmul(ptr_f[:, 192:224], onesf[:, :], rA[:, 32:64],
                                start=True, stop=True)
            rep_tok = ("pe2", p.cnt["pe2"])

            @p.step("vector", waits=[rep_tok], inc="dv2")
            def _(e):
                return e.tensor_copy(out=sm["Cp0rep"][:, :], in_=ptr_f[:, 160:192])

            @p.step("vector", inc="dv2")
            def _(e):
                return e.tensor_copy(out=sm["CCrep"][:, :], in_=ptr_f[:, 192:224])
            tok["ptrf_free"] = ("dv2", p.cnt["dv2"])

            sq_toks = []
            for h in range(NK):
                @p.step("scalar", waits=[ld[("ktm", ch)]], inc="ac2")
                def _(e, h=h, par=par):
                    return e.activation(out=trash[:, :],
                                        in_=ktm2[par][:, h * 128:(h + 1) * 128],
                                        func=AF.Square, accum_out=nk2[:, h:h + 1])
            for h in range(NK):
                @p.step("scalar", waits=[ld[("qtm", ch)]], inc="ac2")
                def _(e, h=h):
                    return e.activation(out=trash[:, :], in_=qtm[:, h * 128:(h + 1) * 128],
                                        func=AF.Square, accum_out=nq2[:, h:h + 1])
            sq_tok = ("ac2", p.cnt["ac2"])
            tok["qtm"] = sq_tok

            @p.step("vector", waits=[sq_tok], inc="dv2")
            def _(e):
                return e.tensor_scalar(out=nk2[:, :], in0=nk2[:, :], scalar1=1e-24,
                                       scalar2=None, op0=ALU.max)

            @p.step("vector", inc="dv2")
            def _(e):
                return e.tensor_scalar(out=nq2[:, :], in0=nq2[:, :], scalar1=1e-24,
                                       scalar2=None, op0=ALU.max)
            mx_tok = ("dv2", p.cnt["dv2"])

            @p.step("scalar", waits=[mx_tok], inc="ac2")
            def _(e):
                return e.activation(out=lnk[:, :], in_=nk2[:, :], func=AF.Ln)

            @p.step("scalar", inc="ac2")
            def _(e):
                return e.activation(out=lnq[:, :], in_=nq2[:, :], func=AF.Ln)
            ln_tok = ("ac2", p.cnt["ac2"])

            def lnbc(t):
                return t[:, :].rearrange("t (h o) -> t h o", o=1).broadcast_to([128, 16, 2])

            def U32(t):
                return t[:, :].rearrange("t (h o) -> t h o", o=2)

            @p.step("vector", waits=[ln_tok, prep_evac_tok], inc="dv2")
            def _(e):
                return e.scalar_tensor_tensor(out=U32(sm["U1"]), in0=lnbc(lnk),
                                              scalar=-0.5, in1=U32(sm["cpTt"]),
                                              op0=ALU.mult, op1=ALU.add)

            @p.step("vector", inc="dv2")
            def _(e):
                return e.tensor_add(out=sm["U1"][:, :], in0=sm["U1"][:, :],
                                    in1=sm["lnbT"][:, :])

            @p.step("vector", inc="dv2")
            def _(e):
                return e.scalar_tensor_tensor(out=U32(sm["U2n"]), in0=lnbc(lnk),
                                              scalar=-0.5, in1=U32(sm["cTt"]),
                                              op0=ALU.mult, op1=ALU.subtract)

            @p.step("vector", inc="dv2")
            def _(e):
                return e.scalar_tensor_tensor(out=U32(sm["U3"]), in0=lnbc(lnq),
                                              scalar=-0.5, in1=U32(sm["cTt"]),
                                              op0=ALU.mult, op1=ALU.add)

            @p.step("vector", waits=[tok["ptrf_free"]], inc="dv2")
            def _(e):
                return e.tensor_sub(out=sm["U3s"][:, :], in0=sm["U3"][:, :],
                                    in1=sm["Cp0rep"][:, :])

            @p.step("vector", inc="dv2")
            def _(e):
                return e.tensor_sub(out=sm["targ"][:, :], in0=sm["U1"][:, :],
                                    in1=sm["Cp0rep"][:, :])
            u1s_tok = ("dv2", p.cnt["dv2"])

            @p.step("scalar", waits=[u1s_tok], inc="ac2")
            def _(e):
                return e.activation(out=sm["negeU1s"][:, :], in_=sm["targ"][:, :],
                                    func=AF.Exp)
            expu_tok = ("ac2", p.cnt["ac2"])

            @p.step("vector", waits=[expu_tok], inc="dv2")
            def _(e):
                return e.tensor_scalar(out=sm["negeU1s"][:, :], in0=sm["negeU1s"][:, :],
                                       scalar1=-1.0, scalar2=None, op0=ALU.mult)
            negu_tok = ("dv2", p.cnt["dv2"])

            @p.step("vector", inc="dv2")
            def _(e):
                return e.tensor_add(out=sm["Edec"][:, :], in0=sm["U2n"][:, :],
                                    in1=sm["CCrep"][:, :])

            @p.step("vector", inc="dv2")
            def _(e):
                return e.tensor_sub(out=sm["Sdec"][:, :], in0=sm["CCrep"][:, :],
                                    in1=sm["Cp0rep"][:, :])
            dec_tok = ("dv2", p.cnt["dv2"])

            @p.step("scalar", waits=[dec_tok], inc="ac2")
            def _(e):
                return e.activation(out=sm["Edec"][:, :], in_=sm["Edec"][:, :], func=AF.Exp)

            @p.step("scalar", inc="ac2")
            def _(e):
                return e.activation(out=sm["Sdec"][:, :], in_=sm["Sdec"][:, :], func=AF.Exp)
            dece_tok = ("ac2", p.cnt["ac2"])

            split_done = []
            for ci, name in enumerate(("U1", "U2n", "U3", "U3s")):
                @p.step("vector", waits=[tok.get("dvb_free")] if ci == 0 else [],
                        inc="dv2")
                def _(e, ci=ci, name=name):
                    return e.tensor_copy(out=Usp[:, ci * 128:ci * 128 + 32],
                                         in_=sm[name][:, :])

                @p.step("vector", inc="dv2")
                def _(e, ci=ci, name=name):
                    return e.tensor_sub(out=Ur1[:, :], in0=sm[name][:, :],
                                        in1=Usp[:, ci * 128:ci * 128 + 32])

                @p.step("vector", inc="dv2")
                def _(e, ci=ci):
                    return e.tensor_copy(out=Usp[:, ci * 128 + 32:ci * 128 + 64],
                                         in_=Ur1[:, :])

                @p.step("vector", inc="dv2")
                def _(e, ci=ci):
                    return e.tensor_sub(out=Ur2[:, :], in0=Ur1[:, :],
                                        in1=Usp[:, ci * 128 + 32:ci * 128 + 64])

                @p.step("vector", inc="dv2")
                def _(e, ci=ci):
                    return e.tensor_copy(out=Usp[:, ci * 128 + 64:ci * 128 + 96],
                                         in_=Ur2[:, :])
            spl_tok = ("dv2", p.cnt["dv2"])

            @p.step("tensor", waits=[spl_tok, prep_evac_tok], inc="pe2")
            def _(e):
                inst = None
                for ci in range(4):
                    inst = e.transpose(ptr_b[:, ci * 128:(ci + 1) * 128],
                                       Usp[:, ci * 128:(ci + 1) * 128], identb[:, :])
                return inst
            utr_tok = ("pe2", p.cnt["pe2"])

            @p.step("vector", waits=[utr_tok, tok.get("xkb_free")], inc="dv2")
            def _(e):
                return e.tensor_copy(out=Utr[:, 0:512], in_=ptr_b[:, 0:512])
            utr_ev = ("dv2", p.cnt["dv2"])
            tok["ptrb_free"] = utr_ev

            for (stg, ci, lo) in ((LU1, 0, 0), (RU2, 1, 3),
                                  (RU1, 0, 3), (RU3, 2, 3)):
                @p.step("sync", waits=[utr_ev, tok["stg"]], inc="d_fl", k=16)
                def _(e, stg=stg, ci=ci, lo=lo):
                    return e.dma_start(
                        out=stg[lo:lo + 3, :],
                        in_=Utr[0:96, ci * 128:(ci + 1) * 128])
            fl_tok = p.full("d_fl")

            gk_toks = {}
            pe_free = {0: tok.get("pe_a_free"), 1: tok.get("pe_b_free")}
            for h in range(NK):
                bank = [pe_a, pe_b][h % 2]
                @p.step("tensor", waits=[ld[("kf", ch)], ld[("qf", ch)],
                                         pe_free[h % 2]], inc="pe2")
                def _(e, h=h, bank=bank, par=par):
                    e.matmul(bank[:, 0:128], kf[par][:, h * 128:(h + 1) * 128],
                             kf[par][:, h * 128:(h + 1) * 128], start=True, stop=True)
                    return e.matmul(bank[:, 128:256], kf[par][:, h * 128:(h + 1) * 128],
                                    qf[par][:, h * 128:(h + 1) * 128],
                                    start=True, stop=True)
                mm_tok = ("pe2", p.cnt["pe2"])

                @p.step("scalar", waits=[mm_tok], inc="ac2")
                def _(e, h=h, bank=bank):
                    return e.copy(out=Gl[:, h * 128:(h + 1) * 128], in_=bank[:, 0:128])

                @p.step("scalar", inc="ac2")
                def _(e, h=h, bank=bank):
                    return e.copy(out=Qu[:, h * 128:(h + 1) * 128], in_=bank[:, 128:256])
                gev = ("ac2", p.cnt["ac2"])
                pe_free[h % 2] = gev

                @p.step("gpsimd", waits=[gev], inc="po2")
                def _(e, h=h):
                    return e.tensor_mul(out=Gu[:, h * 128:(h + 1) * 128],
                                        in0=Gl[:, h * 128:(h + 1) * 128],
                                        in1=maskUs[:, :])

                @p.step("gpsimd", inc="po2")
                def _(e, h=h):
                    return e.tensor_mul(out=Qu[:, h * 128:(h + 1) * 128],
                                        in0=Qu[:, h * 128:(h + 1) * 128],
                                        in1=maskUi[:, :])

                @p.step("gpsimd", inc="po2")
                def _(e, h=h):
                    return e.tensor_mul(out=Gl[:, h * 128:(h + 1) * 128],
                                        in0=Gl[:, h * 128:(h + 1) * 128],
                                        in1=maskL[:, :])
            gmask_tok = ("po2", p.cnt["po2"])

            etypes = [(Aw, LU1, RU2, Gl), (ATw, LU2, RU1, Gu),
                      (Wow, LU2, RU3, Qu), (R3w, None, RU3, None)]
            eidx = 0
            etmp_free = [None, None]
            for (wide, lst, rst, comb) in etypes:
                if wide is ATw:
                    @p.step("sync", waits=[("pe2", p.cnt["pe2"]), utr_ev],
                            inc="d_fl", k=16)
                    def _(e):
                        return e.dma_start(out=LU1[0:3, :],
                                           in_=Utr[0:96, 1 * 128:2 * 128])
                    fl_tok = p.full("d_fl")
                if wide is R3w:
                    @p.step("sync", waits=[("pe2", p.cnt["pe2"]), utr_ev],
                            inc="d_fl", k=16)
                    def _(e):
                        return e.dma_start(out=RU3[3:6, :],
                                           in_=Utr[0:96, 3 * 128:4 * 128])
                    fl_tok = p.full("d_fl")
                for gq in range(8):
                    bank = [pe_a, pe_b][gq % 2]
                    Etmp = Etmp2[eidx % 2]
                    @p.step("tensor", waits=[fl_tok, lz_tok, pe_free[gq % 2]],
                            inc="pe2")
                    def _(e, wide=wide, lst=lst, rst=rst, gq=gq, bank=bank):
                        inst = None
                        for i in range(4):
                            hp = gq * 4 + i
                            lhsT = Lz[:, :] if lst is None else lst[:, hp * 128:(hp + 1) * 128]
                            inst = e.matmul(bank[:, i * 128:(i + 1) * 128], lhsT,
                                            rst[:, hp * 128:(hp + 1) * 128],
                                            start=True, stop=True)
                        return inst
                    emm_tok = ("pe2", p.cnt["pe2"])

                    if eidx % 2 == 0:
                        @p.step("vector", waits=[emm_tok, etmp_free[eidx % 2]],
                                inc="dv2")
                        def _(e, bank=bank, Etmp=Etmp):
                            return e.tensor_scalar(out=Etmp[:, :], in0=bank[:, :],
                                                   scalar1=CL, scalar2=None,
                                                   op0=ALU.min)
                        cl_tok = ("dv2", p.cnt["dv2"])
                        pe_free[gq % 2] = cl_tok

                        @p.step("scalar", waits=[cl_tok], inc="ac2")
                        def _(e, wide=wide, gq=gq, Etmp=Etmp):
                            return e.activation(out=wide[:, gq * 512:(gq + 1) * 512],
                                                in_=Etmp[:, :], func=AF.Exp)
                        ex_tok = ("ac2", p.cnt["ac2"])
                    else:
                        @p.step("scalar", waits=[emm_tok, etmp_free[eidx % 2]],
                                inc="ac2")
                        def _(e, bank=bank, Etmp=Etmp):
                            return e.activation(out=Etmp[:, :], in_=bank[:, :],
                                                func=AF.Relu, scale=-1.0,
                                                bias=cl60[:, :])
                        cl_tok = ("ac2", p.cnt["ac2"])
                        pe_free[gq % 2] = cl_tok

                        @p.step("scalar", inc="ac2")
                        def _(e, wide=wide, gq=gq, Etmp=Etmp):
                            return e.activation(out=wide[:, gq * 512:(gq + 1) * 512],
                                                in_=Etmp[:, :], func=AF.Exp,
                                                scale=-1.0, bias=cl60[:, :])
                        ex_tok = ("ac2", p.cnt["ac2"])
                    etmp_free[eidx % 2] = ex_tok
                    eidx += 1

                    if comb is not None:
                        @p.step("vector", waits=[ex_tok, gmask_tok], inc="dv2")
                        def _(e, wide=wide, comb=comb, gq=gq):
                            cb = comb[:, gq * 256:(gq + 1) * 256] \
                                .rearrange("p (h o t) -> p h o t", o=1, t=128) \
                                .broadcast_to([128, 2, 2, 128])
                            return e.tensor_tensor(
                                out=wide[:, gq * 512:(gq + 1) * 512]
                                    .rearrange("p (h o t) -> p h o t", o=2, t=128),
                                in0=wide[:, gq * 512:(gq + 1) * 512]
                                    .rearrange("p (h o t) -> p h o t", o=2, t=128),
                                in1=cb, op=ALU.mult)
            ew_tok = ("dv2", p.cnt["dv2"])
            ew_ac = ("ac2", p.cnt["ac2"])
            tok["stg"] = ("pe2", p.cnt["pe2"])

            banks = [px, ptr_f, py, ps_a, ps_b, pe_a, pe_b]
            bankf = tok.get("bankf", {id(b): None for b in banks})
            bidx = tok.get("bidx", 0)

            def next_bank():
                nonlocal bidx
                b = banks[bidx % len(banks)]
                bidx += 1
                return b

            wv = tok.get("wv", {})

            u_toks = []
            for gq in range(8):
                b = next_bank()
                h0, h1 = 2 * gq, 2 * gq + 1
                @p.step("tensor", waits=[ld[("kf", ch)], ew_tok, ew_ac,
                                         bankf[id(b)], tok["Sb_w"],
                                         wv.get("xkbw_last")], inc="pe2")
                def _(e, h0=h0, h1=h1, par=par, b=b):
                    e.matmul(b[:, 0:256], kf[par][:, h0 * 128:(h0 + 1) * 128],
                             Sb[:, h0 * 256:(h0 + 1) * 256], start=True, stop=True)
                    return e.matmul(b[:, 256:512],
                                    kf[par][:, h1 * 128:(h1 + 1) * 128],
                                    Sb[:, h1 * 256:(h1 + 1) * 256],
                                    start=True, stop=True)
                t_pe = ("pe2", p.cnt["pe2"])

                @p.step("vector", waits=[ld[("vtm", ch)], prep_evac_tok], inc="dv2")
                def _(e, gq=gq):
                    inst = None
                    for i in range(4):
                        hp = gq * 4 + i
                        inst = e.tensor_scalar(
                            out=ubv[:, i * 128:(i + 1) * 128],
                            in0=vtm[:, hp * 128:(hp + 1) * 128],
                            scalar1=sm["bTf"][:, hp:hp + 1], scalar2=None,
                            op0=ALU.mult)
                    return inst

                @p.step("vector", waits=[t_pe, negu_tok], inc="dv2")
                def _(e, gq=gq, b=b):
                    inst = None
                    for i in range(4):
                        hp = gq * 4 + i
                        inst = e.scalar_tensor_tensor(
                            out=xkbw[:, gq * 512 + i * 128:gq * 512 + (i + 1) * 128],
                            in0=b[:, i * 128:(i + 1) * 128],
                            scalar=sm["negeU1s"][:, hp:hp + 1],
                            in1=ubv[:, i * 128:(i + 1) * 128],
                            op0=ALU.mult, op1=ALU.add)
                    return inst
                bankf[id(b)] = ("dv2", p.cnt["dv2"])
                u_toks.append(bankf[id(b)])

            tok["kf"][0] = tok["kf"][1] = ("pe2", p.cnt["pe2"])
            tok["vtm"] = ("dv2", p.cnt["dv2"])
            ap_tok = []
            for gq in range(8):
                b = next_bank()
                @p.step("tensor", waits=[u_toks[gq], bankf[id(b)]], inc="pe2")
                def _(e, gq=gq, b=b):
                    inst = None
                    for i in range(4):
                        hp = gq * 4 + i
                        inst = e.matmul(b[:, i * 128:(i + 1) * 128],
                                        ATw[:, hp * 128:(hp + 1) * 128],
                                        xkbw[:, gq * 512 + i * 128:gq * 512 + (i + 1) * 128],
                                        start=True, stop=True)
                    return inst
                t_pe = ("pe2", p.cnt["pe2"])

                @p.step("vector", waits=[t_pe], inc="dv2")
                def _(e, gq=gq, b=b):
                    s = xkbw[:, gq * 512:(gq + 1) * 512]
                    return e.tensor_add(out=s, in0=b[:, :], in1=s)
                bankf[id(b)] = ("dv2", p.cnt["dv2"])
                ap_tok.append(bankf[id(b)])

            Aping = [Aw, Aw2]
            Tping = [ATw, ATw2]
            sA_prev = [None] * 8
            sT_prev = [None] * 8
            for lev in range(1, 7):
                half = lev % 2
                sA_t = []
                sT_t = []
                for gq in range(8):
                    bA = next_bank() if lev < 6 else None
                    bT = next_bank()
                    @p.step("tensor", waits=[sA_prev[gq], sT_prev[gq],
                                             None if bA is None else bankf[id(bA)],
                                             bankf[id(bT)]] +
                                            ([ap_tok[gq]] if lev == 1 else []),
                            inc="pe2")
                    def _(e, gq=gq, lev=lev, bA=bA, bT=bT, half=half):
                        inst = None
                        for i in range(4):
                            hp = gq * 4 + i
                            sl = slice(i * 128, (i + 1) * 128)
                            if lev == 1:
                                a = Aw[:, hp * 128:(hp + 1) * 128]
                                at = ATw[:, hp * 128:(hp + 1) * 128]
                            else:
                                ph = (lev - 1) % 2
                                a = Aping[ph][:, hp * 128:(hp + 1) * 128]
                                at = Tping[ph][:, hp * 128:(hp + 1) * 128]
                            if bA is not None:
                                e.matmul(bA[:, sl], at, a, start=True, stop=True)
                            inst = e.matmul(bT[:, sl], a, at, start=True, stop=True)
                        return inst
                    t_sq = ("pe2", p.cnt["pe2"])

                    if bA is not None:
                        @p.step("scalar", waits=[t_sq], inc="ac2")
                        def _(e, gq=gq, bA=bA, half=half):
                            return e.copy(out=Aping[half][:, gq * 512:(gq + 1) * 512],
                                          in_=bA[:, :])
                        sA_t.append(("ac2", p.cnt["ac2"]))
                        bankf[id(bA)] = sA_t[-1]
                    else:
                        sA_t.append(None)
                    @p.step("vector", waits=[t_sq], inc="dv2")
                    def _(e, gq=gq, bT=bT, half=half):
                        return e.tensor_copy(out=Tping[half][:, gq * 512:(gq + 1) * 512],
                                             in_=bT[:, :])
                    sT_t.append(("dv2", p.cnt["dv2"]))
                    bankf[id(bT)] = sT_t[-1]
                sA_prev = sA_t
                sT_prev = sT_t

                napt = []
                for gq in range(8):
                    b = next_bank()
                    use_ident = (gq % 2 == 1)
                    @p.step("tensor", waits=[sT_t[gq], ap_tok[gq], bankf[id(b)]],
                            inc="pe2")
                    def _(e, gq=gq, b=b, half=half, use_ident=use_ident):
                        inst = None
                        for i in range(4):
                            hp = gq * 4 + i
                            sl = slice(i * 128, (i + 1) * 128)
                            xs = xkbw[:, gq * 512 + i * 128:gq * 512 + (i + 1) * 128]
                            if use_ident:
                                e.matmul(b[:, sl], identb[:, :], xs,
                                         start=True, stop=False)
                                inst = e.matmul(b[:, sl],
                                                Tping[half][:, hp * 128:(hp + 1) * 128],
                                                xs, start=False, stop=True)
                            else:
                                inst = e.matmul(b[:, sl],
                                                Tping[half][:, hp * 128:(hp + 1) * 128],
                                                xs, start=True, stop=True)
                        return inst
                    t_ap = ("pe2", p.cnt["pe2"])

                    if use_ident:
                        @p.step("scalar", waits=[t_ap], inc="ac2")
                        def _(e, gq=gq, b=b):
                            return e.copy(out=xkbw[:, gq * 512:(gq + 1) * 512],
                                          in_=b[:, :])
                        bankf[id(b)] = ("ac2", p.cnt["ac2"])
                    else:
                        @p.step("vector", waits=[t_ap], inc="dv2")
                        def _(e, gq=gq, b=b):
                            s = xkbw[:, gq * 512:(gq + 1) * 512]
                            return e.tensor_add(out=s, in0=b[:, :], in1=s)
                        bankf[id(b)] = ("dv2", p.cnt["dv2"])
                    napt.append(bankf[id(b)])
                ap_tok = napt

            og_done_g = None
            o_toks = []
            pend = []
            for gq in range(8):
                b = next_bank()
                qscL = Lqsc[gq % 2]
                obfL = Lobf[gq % 2]
                rofL, robL = Lrof[gq % 2], Lrob[gq % 2]
                @p.step("vector", waits=[ew_tok, ew_ac, ld[("qf", ch)],
                                         wv.get(("oq", gq % 2))], inc="dv2")
                def _(e, gq=gq, par=par, qscL=qscL):
                    qb = qf[par][:, 2 * gq * 128:(2 * gq + 2) * 128] \
                        .rearrange("p (h o t) -> p h o t", o=1, t=128) \
                        .broadcast_to([128, 2, 2, 128])
                    return e.tensor_tensor(
                        out=qscL[:, :].rearrange("p (h o t) -> p h o t", o=2, t=128),
                        in0=qb,
                        in1=R3w[:, gq * 512:(gq + 1) * 512]
                            .rearrange("p (h o t) -> p h o t", o=2, t=128),
                        op=ALU.mult)
                t_qsc = ("dv2", p.cnt["dv2"])

                @p.step("tensor", waits=[t_qsc, ap_tok[gq], bankf[id(b)]], inc="pe2")
                def _(e, gq=gq, b=b, qscL=qscL):
                    inst = None
                    for i in range(4):
                        hp = gq * 4 + i
                        h, pp = hp // 2, hp % 2
                        sl = slice(i * 128, (i + 1) * 128)
                        e.matmul(b[:, sl],
                                 xkbw[:, gq * 512 + i * 128:gq * 512 + (i + 1) * 128],
                                 Wow[:, hp * 128:(hp + 1) * 128],
                                 start=True, stop=False)
                        inst = e.matmul(
                            b[:, sl],
                            Sb[:, h * 256 + pp * 128:h * 256 + (pp + 1) * 128],
                            qscL[:, sl], start=False, stop=True)
                    return inst
                t_omm = ("pe2", p.cnt["pe2"])

                @p.step("scalar", waits=[t_omm], inc="ac2")
                def _(e, b=b, obfL=obfL):
                    return e.copy(out=obfL[:, :], in_=b[:, :])
                t_obf = ("ac2", p.cnt["ac2"])

                @p.step("vector", waits=[t_obf], inc="dv2")
                def _(e, obfL=obfL, qscL=qscL):
                    return e.tensor_mul(out=qscL[:, :], in0=obfL[:, :], in1=obfL[:, :])
                t_osq = ("dv2", p.cnt["dv2"])

                @p.step("tensor", waits=[t_osq], inc="pe2")
                def _(e, b=b, qscL=qscL):
                    return e.matmul(b[0:1, 0:512], onescb[:, :], qscL[:, :],
                                    start=True, stop=True)
                t_row = ("pe2", p.cnt["pe2"])

                @p.step("vector", waits=[t_row], inc="dv2")
                def _(e, b=b, rofL=rofL):
                    return e.tensor_scalar(out=rofL[:, :], in0=b[0:1, 0:512],
                                           scalar1=1.0 / 128, scalar2=EPS_RMS,
                                           op0=ALU.mult, op1=ALU.add)
                t_ro = ("dv2", p.cnt["dv2"])

                @p.step("scalar", waits=[t_ro], inc="ac2")
                def _(e, rofL=rofL):
                    return e.activation(out=rofL[:, :], in_=rofL[:, :], func=AF.Ln)

                @p.step("scalar", inc="ac2")
                def _(e, rofL=rofL):
                    return e.activation(out=rofL[:, :], in_=rofL[:, :],
                                        func=AF.Exp, scale=-0.5)
                t_rl = ("ac2", p.cnt["ac2"])

                @p.step("vector", waits=[t_rl], inc="dv2")
                def _(e, rofL=rofL, robL=robL):
                    return e.tensor_copy(out=robL[:, :], in_=rofL[:, :])
                t_rb = ("dv2", p.cnt["dv2"])

                @p.step("tensor", waits=[t_rb, t_ro], inc="pe2")
                def _(e, b=b, robL=robL):
                    return e.matmul(b[:, 0:512], onesb[:, :], robL[:, :],
                                    start=True, stop=True)
                t_rrep = ("pe2", p.cnt["pe2"])

                @p.step("vector", waits=[t_rrep, ld[("zf", ch)]], inc="dv2")
                def _(e, b=b, obfL=obfL):
                    return e.tensor_mul(out=obfL[:, :], in0=obfL[:, :], in1=b[:, 0:512])
                @p.step("vector", inc="dv2")
                def _(e, obfL=obfL):
                    return e.tensor_scalar(out=obfL[:, :], in0=obfL[:, :],
                                           scalar1=normw[:, :], scalar2=None,
                                           op0=ALU.mult)

                @p.step("vector", waits=[tok["ogb"][par]], inc="dv2")
                def _(e, gq=gq, par=par, obfL=obfL):
                    return e.tensor_mul(out=ogb[par][:, gq * 512:(gq + 1) * 512],
                                        in0=obfL[:, :],
                                        in1=zf[:, gq * 512:(gq + 1) * 512])
                og_done_g = ("dv2", p.cnt["dv2"])
                bankf[id(b)] = og_done_g
                wv[("oq", gq % 2)] = og_done_g
                o_toks.append(og_done_g)

            for gq in range(8):
                b = next_bank()
                kdeL = Lkde[gq % 2]
                @p.step("vector", waits=[dece_tok, ld[("ktm", ch)],
                                         wv.get(("sk", gq % 2))], inc="dv2")
                def _(e, gq=gq, kdeL=kdeL, par=par):
                    kb = ktm2[par][:, 2 * gq * 128:(2 * gq + 2) * 128] \
                        .rearrange("p (h o t) -> p h o t", o=1, t=128) \
                        .broadcast_to([128, 2, 2, 128])
                    eb = sm["Edec"][:, 4 * gq:4 * gq + 4] \
                        .rearrange("p (hp o) -> p hp o", o=1) \
                        .broadcast_to([128, 4, 128]) \
                        .rearrange("p (h pp) t -> p h pp t", pp=2)
                    return e.tensor_tensor(
                        out=kdeL[:, :].rearrange("p (h o t) -> p h o t", o=2, t=128),
                        in0=kb, in1=eb, op=ALU.mult)
                t_kd = ("dv2", p.cnt["dv2"])

                @p.step("tensor", waits=[t_kd, o_toks[gq], bankf[id(b)]], inc="pe2")
                def _(e, gq=gq, b=b, kdeL=kdeL):
                    inst = None
                    for i in range(4):
                        sl = slice(i * 128, (i + 1) * 128)
                        inst = e.matmul(b[:, sl], kdeL[:, sl],
                                        xkbw[:, gq * 512 + i * 128:gq * 512 + (i + 1) * 128],
                                        start=True, stop=True)
                    return inst
                t_smm = ("pe2", p.cnt["pe2"])
                wv[("sk", gq % 2)] = t_smm

                @p.step("vector", waits=[t_smm], inc="dv2")
                def _(e, gq=gq, b=b):
                    inst = None
                    for i in range(4):
                        hp = gq * 4 + i
                        h, pp = hp // 2, hp % 2
                        ssl = slice(h * 256 + pp * 128, h * 256 + (pp + 1) * 128)
                        inst = e.scalar_tensor_tensor(
                            out=Sb[:, ssl], in0=Sb[:, ssl],
                            scalar=sm["Sdec"][:, hp:hp + 1],
                            in1=b[:, i * 128:(i + 1) * 128],
                            op0=ALU.mult, op1=ALU.add)
                    return inst
                t_sev = ("dv2", p.cnt["dv2"])
                bankf[id(b)] = t_sev
                tok["Sb_w"] = t_sev
            wv["xkbw_last"] = ("pe2", p.cnt["pe2"])
            tok["wv"] = wv
            tok["bankf"] = bankf
            tok["bidx"] = bidx
            tok["dvb_free"] = ("pe2", p.cnt["pe2"])
            tok["xkb_free"] = ("pe2", p.cnt["pe2"])
            tok["ptrf_free"] = bankf[id(ptr_f)]
            tok["pe_a_free"] = bankf[id(pe_a)]
            tok["pe_b_free"] = bankf[id(pe_b)]

            tok["ktm"] = ("dv2", p.cnt["dv2"])
            tok["zf"] = ("dv2", p.cnt["dv2"])
            tok["qf"][par] = ("dv2", p.cnt["dv2"])

            @p.step("sync", waits=[og_done_g], inc="d_og", k=16)
            def _(e, par=par, t0=t0):
                return e.dma_start(
                    out=og_d[:, t0:t0 + 128].rearrange("(hp p) t -> p hp t", p=128),
                    in_=ogb[par][:, :].rearrange("p (hp t) -> p hp t", t=128))
            tok["ogb"][0] = tok["ogb"][1] = p.full("d_og")

            if ch + 1 < NCH:
                chunk_loads(ch + 1)

        g["p2_bar"] = [("pe2", p.cnt["pe2"]), ("dv2", p.cnt["dv2"]),
                       ("ac2", p.cnt["ac2"]), p.full("d_og")]



def emit_p25(nc, p, ctx, env):
    g = env
    og_d, x1f_d, xT_d, woutT = g["og_d"], g["x1f_d"], g["xT"], g["woutT"]
    onesf, onescb = g["onesf"], g["onescb"]
    bar = g["p2_bar"]

    sc = ExitStack()
    with sc:
        ee = sc.enter_context
        ogr = ee(nc.sbuf_tensor("ogr", [128, NFZ * T], BF16))
        wo = [ee(nc.sbuf_tensor(f"wo{i}", [128, NFZ * 128], BF16)) for i in range(2)]
        xb = [ee(nc.sbuf_tensor(f"xb{i}", [128, 512], F32)) for i in range(2)]
        rt2 = [ee(nc.sbuf_tensor(f"rt{i}", [128, 512], F32)) for i in range(2)]
        rb = ee(nc.sbuf_tensor("rb", [128, 512], BF16))
        rsq = ee(nc.sbuf_tensor("rsq", [128, 512], BF16))
        rb2b = [rb, ee(nc.sbuf_tensor("rbB", [128, 512], BF16))]
        rsq2b = [rsq, ee(nc.sbuf_tensor("rsqB", [128, 512], BF16))]
        srow = ee(nc.sbuf_tensor("srow", [1, T], F32))
        qrow_ = ee(nc.sbuf_tensor("qrow_", [1, T], F32))
        mrow = ee(nc.sbuf_tensor("mrow", [1, 512], F32))
        vrow = ee(nc.sbuf_tensor("vrow", [1, 512], F32))
        rpass = [ee(nc.sbuf_tensor(f"rpass{i}", [128, 512], F32)) for i in range(2)]
        py_ = [ee(nc.psum_tensor(f"p25y{i}", [128, 512], F32)) for i in range(2)]
        pst = ee(nc.psum_tensor("pst", [1, 1024], F32))
        prp1 = ee(nc.psum_tensor("prp1", [128, 512], F32))
        prp2 = ee(nc.psum_tensor("prp2", [128, 512], F32))

        for q in range(4):
            @p.step("gpsimd", waits=bar if q == 0 else [], inc="d_ogr", k=16)
            def _(e, q=q):
                return e.dma_start(
                    out=ogr[:, :].rearrange("p (v t) -> p v t", t=T)[:, q * 8:(q + 1) * 8],
                    in_=og_d[q * 1024:(q + 1) * 1024, :]
                        .rearrange("(v p) t -> p v t", p=128))
        ogr_tok = p.full("d_ogr")

        @p.step("vector", waits=bar, inc="dv3")
        def _(e):
            return e.memset(srow[:, :], 0.0)

        @p.step("vector", inc="dv3")
        def _(e):
            return e.memset(qrow_[:, :], 0.0)
        srow_z = ("dv3", p.cnt["dv3"])

        wol_tok = [None, None]
        wof_tok = [None, None]
        xf_tok = [None, None]
        pyf = [None, None]
        pst_free = None
        statq = []
        rbfree = [None, None]

        def wo_load(cc):
            i = cc % 2
            @p.step("gpsimd", waits=[wof_tok[i]] + (bar if cc == 0 else []),
                    inc=f"d_wo{i}", k=16)
            def _(e, cc=cc, i=i):
                return e.dma_start(
                    out=wo[i][:, :].rearrange("p (v c) -> p v c", c=128),
                    in_=woutT[:, cc * 128:(cc + 1) * 128]
                        .rearrange("(v p) c -> p v c", p=128))
            wol_tok[i] = p.full(f"d_wo{i}")

        wo_load(0)
        wo_load(1)

        for cc in range(NCC):
            i = cc % 2
            for ts in range(4):
                bank = py_[ts % 2]
                @p.step("scalar", waits=[xf_tok[i]] + (bar if cc < 2 and ts == 0 else []),
                        inc=f"d_xr{i}", k=16)
                def _(e, cc=cc, ts=ts, i=i):
                    return e.dma_start(out=xb[i][:, :],
                                       in_=xT_d[cc * 128:(cc + 1) * 128,
                                                ts * 512:(ts + 1) * 512])
                xld = p.full(f"d_xr{i}")

                @p.step("tensor", waits=[wol_tok[i], ogr_tok, pyf[ts % 2]],
                        inc="pe3")
                def _(e, cc=cc, ts=ts, i=i, bank=bank):
                    inst = None
                    for v in range(NFZ):
                        inst = e.matmul(bank[:, :], wo[i][:, v * 128:(v + 1) * 128],
                                        ogr[:, v * T + ts * 512:v * T + (ts + 1) * 512],
                                        start=(v == 0), stop=(v == NFZ - 1))
                    return inst
                ymm = ("pe3", p.cnt["pe3"])
                if ts == 3:
                    wof_tok[i] = ymm
                    if cc + 2 < NCC:
                        wo_load(cc + 2)

                rpar = (cc * 4 + ts) % 2
                rt = rt2[rpar]
                @p.step("vector", waits=[ymm, xld, p.full(f"d_r{rpar}")], inc="dv3")
                def _(e, i=i, bank=bank, rt=rt):
                    return e.tensor_add(out=rt[:, :], in0=bank[:, :], in1=xb[i][:, :])
                radd = ("dv3", p.cnt["dv3"])
                pyf[ts % 2] = radd
                xf_tok[i] = radd

                @p.step("sync", waits=[radd], inc=f"d_r{rpar}", k=16)
                def _(e, cc=cc, ts=ts, rt=rt):
                    return e.dma_start(out=x1f_d[cc * 128:(cc + 1) * 128,
                                                 ts * 512:(ts + 1) * 512],
                                       in_=rt[:, :])

                spar = (cc * 4 + ts) % 2
                rbp = rb2b[spar]
                rsqp = rsq2b[spar]
                @p.step("vector", waits=[rbfree[spar]], inc="dv3")
                def _(e, rt=rt, rbp=rbp):
                    return e.tensor_copy(out=rbp[:, :], in_=rt[:, :])

                @p.step("vector", inc="dv3")
                def _(e, rbp=rbp, rsqp=rsqp):
                    return e.tensor_mul(out=rsqp[:, :], in0=rbp[:, :], in1=rbp[:, :])
                rsq_tok = ("dv3", p.cnt["dv3"])
                statq.append((ts, rbp, rsqp, rsq_tok, spar))

                if len(statq) == 2:
                    (fts, frb, frsq, ftok, fpar) = statq.pop(0)
                    @p.step("tensor", waits=[ftok, pst_free], inc="pe3")
                    def _(e, frb=frb, frsq=frsq):
                        e.matmul(pst[:, 0:512], onescb[:, :], frb[:, :],
                                 start=True, stop=True)
                        return e.matmul(pst[:, 512:1024], onescb[:, :], frsq[:, :],
                                        start=True, stop=True)
                    srow_mm = ("pe3", p.cnt["pe3"])

                    @p.step("vector", waits=[srow_mm, srow_z], inc="dv3")
                    def _(e, fts=fts):
                        return e.tensor_add(out=srow[:, fts * 512:(fts + 1) * 512],
                                            in0=srow[:, fts * 512:(fts + 1) * 512],
                                            in1=pst[0:1, 0:512])

                    @p.step("vector", inc="dv3")
                    def _(e, fts=fts):
                        return e.tensor_add(out=qrow_[:, fts * 512:(fts + 1) * 512],
                                            in0=qrow_[:, fts * 512:(fts + 1) * 512],
                                            in1=pst[0:1, 512:1024])
                    pst_free = ("dv3", p.cnt["dv3"])
                    rbfree[fpar] = srow_mm
        while statq:
            (fts, frb, frsq, ftok, fpar) = statq.pop(0)
            @p.step("tensor", waits=[ftok, pst_free], inc="pe3")
            def _(e, frb=frb, frsq=frsq):
                e.matmul(pst[:, 0:512], onescb[:, :], frb[:, :], start=True, stop=True)
                return e.matmul(pst[:, 512:1024], onescb[:, :], frsq[:, :],
                                start=True, stop=True)
            srow_mm = ("pe3", p.cnt["pe3"])

            @p.step("vector", waits=[srow_mm, srow_z], inc="dv3")
            def _(e, fts=fts):
                return e.tensor_add(out=srow[:, fts * 512:(fts + 1) * 512],
                                    in0=srow[:, fts * 512:(fts + 1) * 512],
                                    in1=pst[0:1, 0:512])

            @p.step("vector", inc="dv3")
            def _(e, fts=fts):
                return e.tensor_add(out=qrow_[:, fts * 512:(fts + 1) * 512],
                                    in0=qrow_[:, fts * 512:(fts + 1) * 512],
                                    in1=pst[0:1, 512:1024])
            pst_free = ("dv3", p.cnt["dv3"])
        passA_dv = ("dv3", p.cnt["dv3"])
        passA_r = [p.full("d_r0"), p.full("d_r1")]

        rp_free = [None, None]
        rp_ld = [None, None]
        prp_free = None
        for ts in range(4):
            @p.step("vector", waits=[passA_dv], inc="dv3")
            def _(e, ts=ts):
                return e.tensor_scalar(out=mrow[:, :], in0=srow[0:1, ts * 512:(ts + 1) * 512],
                                       scalar1=1.0 / HID, scalar2=None, op0=ALU.mult)

            @p.step("vector", inc="dv3")
            def _(e, ts=ts):
                return e.tensor_scalar(out=vrow[:, :], in0=qrow_[0:1, ts * 512:(ts + 1) * 512],
                                       scalar1=1.0 / HID, scalar2=None, op0=ALU.mult)

            @p.step("vector", inc="dv3")
            def _(e):
                return e.tensor_mul(out=rb[0:1, 0:512], in0=mrow[:, :], in1=mrow[:, :])

            @p.step("vector", inc="dv3")
            def _(e):
                return e.tensor_sub(out=vrow[:, :], in0=vrow[:, :], in1=rb[0:1, 0:512])

            @p.step("vector", inc="dv3")
            def _(e):
                return e.tensor_scalar(out=vrow[:, :], in0=vrow[:, :],
                                       scalar1=LN_EPS, scalar2=None, op0=ALU.add)
            var_tok = ("dv3", p.cnt["dv3"])

            @p.step("scalar", waits=[var_tok], inc="ac3")
            def _(e):
                return e.activation(out=vrow[:, :], in_=vrow[:, :], func=AF.Ln)

            @p.step("scalar", inc="ac3")
            def _(e):
                return e.activation(out=vrow[:, :], in_=vrow[:, :], func=AF.Exp,
                                    scale=-0.5)
            rstd_tok = ("ac3", p.cnt["ac3"])

            @p.step("tensor", waits=[rstd_tok, prp_free], inc="pe3")
            def _(e):
                e.matmul(prp1[:, :], onesf[:, :], mrow[:, :], start=True, stop=True)
                return e.matmul(prp2[:, :], onesf[:, :], vrow[:, :],
                                start=True, stop=True)
            rep_tok = ("pe3", p.cnt["pe3"])

            for cc in range(NCC):
                i = cc % 2
                @p.step("scalar", waits=passA_r + [rp_free[i]], inc=f"d_rp{i}", k=16)
                def _(e, cc=cc, ts=ts, i=i):
                    return e.dma_start(out=rpass[i][:, :],
                                       in_=x1f_d[cc * 128:(cc + 1) * 128,
                                                 ts * 512:(ts + 1) * 512])
                rld = p.full(f"d_rp{i}")

                @p.step("vector", waits=[rld, rep_tok], inc="dv3")
                def _(e, i=i):
                    return e.tensor_sub(out=rpass[i][:, :], in0=rpass[i][:, :],
                                        in1=prp1[:, :])

                @p.step("vector", inc="dv3")
                def _(e, i=i):
                    return e.tensor_mul(out=rpass[i][:, :], in0=rpass[i][:, :],
                                        in1=prp2[:, :])
                x1done = ("dv3", p.cnt["dv3"])
                prp_free = x1done

                @p.step("sync", waits=[x1done], inc=f"d_x1{i}", k=16)
                def _(e, cc=cc, ts=ts, i=i):
                    return e.dma_start(out=x1f_d[cc * 128:(cc + 1) * 128,
                                                 ts * 512:(ts + 1) * 512],
                                       in_=rpass[i][:, :])
                rp_free[i] = p.full(f"d_x1{i}")

        g["p25_bar"] = [("pe3", p.cnt["pe3"]), ("dv3", p.cnt["dv3"]),
                        ("ac3", p.cnt["ac3"]), p.full("d_x10"), p.full("d_x11")]


def emit_p34(nc, p, ctx, env):
    g = env
    x1f_d, upT, downT, outT_d = g["x1f_d"], g["upT"], g["downT"], g["outT"]
    upsc, dssc, dseff, oscal = g["upsc"], g["dssc"], g["dseff"], g["oscal"]
    onesf, onescb = g["onesf"], g["onescb"]
    bar = g["p25_bar"]
    TH = T // 2

    sc = ExitStack()
    with sc:
        ee = sc.enter_context
        x1b = ee(nc.sbuf_tensor("x1b", [128, NCC * TH], BF16))
        y2 = ee(nc.sbuf_tensor("y2", [128, NCC * TH], F32))
        h1g = ee(nc.sbuf_tensor("h1g", [128, 8 * TH], BF16))
        upf = [ee(nc.sbuf_tensor(f"upf{i}", [128, NCC * 128], BF16)) for i in range(2)]
        upb2 = [ee(nc.sbuf_tensor(f"upb{i}", [128, NCC * 128], BF16)) for i in range(2)]
        dnf = [ee(nc.sbuf_tensor(f"dnf{i}", [128, HID], BF16)) for i in range(2)]
        dnb8 = ee(nc.sbuf_tensor("dnb8", [128, 8 * HID], BF16))
        xr = [ee(nc.sbuf_tensor(f"xr{i}", [128, 512], F32)) for i in range(2)]
        rb4 = ee(nc.sbuf_tensor("rb4", [128, 512], BF16))
        rsq4 = ee(nc.sbuf_tensor("rsq4", [128, 512], BF16))
        mrow4 = ee(nc.sbuf_tensor("mrow4", [1, 512], F32))
        vrow4 = ee(nc.sbuf_tensor("vrow4", [1, 512], F32))
        pu = [ee(nc.psum_tensor(f"p34u{i}", [128, 512], F32)) for i in range(2)]
        pd = [ee(nc.psum_tensor(f"p34d{i}", [128, 512], F32)) for i in range(2)]
        ps1 = ee(nc.psum_tensor("p34s1", [1, 512], F32))
        ps2 = ee(nc.psum_tensor("p34s2", [1, 512], F32))
        pr1 = ee(nc.psum_tensor("p34r1", [128, 512], F32))
        pr2 = ee(nc.psum_tensor("p34r2", [128, 512], F32))

        @p.step("tensor", waits=bar, inc="pe4")
        def _(e):
            return e.matmul(pr1[:, 0:1], onesf[:, :], oscal[:, :],
                            start=True, stop=True)
        osr_tok = ("pe4", p.cnt["pe4"])

        @p.step("vector", waits=[osr_tok], inc="dv4")
        def _(e):
            return e.tensor_scalar(out=dseff[:, :], in0=dssc[:, :],
                                   scalar1=pr1[:, 0:1], scalar2=None, op0=ALU.mult)
        dse_tok = ("dv4", p.cnt["dv4"])

        tokp = {}
        for hf in range(2):
            th0 = hf * TH
            for cc in range(NCC):
                @p.step("gpsimd", waits=(bar if (hf == 0 and cc == 0) else
                                         [tokp.get("x1b_free")] if cc == 0 else []),
                        inc="d_x1b", k=16)
                def _(e, cc=cc, th0=th0):
                    return e.dma_start(out=x1b[:, cc * TH:(cc + 1) * TH],
                                       in_=x1f_d[cc * 128:(cc + 1) * 128,
                                                 th0:th0 + TH])
            x1b_tok = p.full("d_x1b")

            @p.step("vector", waits=[tokp.get("y2_free")], inc="dv4")
            def _(e):
                return e.memset(y2[:, :], 0.0)
            y2z_tok = ("dv4", p.cnt["dv4"])

            half_ac = ("ac4", p.cnt["ac4"])
            half_pe = ("pe4", p.cnt["pe4"])
            upl_tok = [None, None]
            upf_free = [half_ac, half_ac]
            dnl_tok = [None, None]
            dnf_free = [half_ac, half_ac]
            dnb_free = half_pe

            def up_load(hc):
                i = hc % 2
                @p.step("gpsimd", waits=[upf_free[i]], inc=f"d_up{i}", k=16)
                def _(e, hc=hc, i=i):
                    return e.dma_start(
                        out=upf[i][:, :].rearrange("p (cc f) -> p cc f", f=128),
                        in_=upT[:, hc * 128:(hc + 1) * 128]
                            .rearrange("(cc p) f -> p cc f", p=128))
                upl_tok[i] = p.full(f"d_up{i}")

            def dn_load(hc):
                i = hc % 2
                @p.step("gpsimd", waits=[dnf_free[i]], inc=f"d_dn{i}", k=16)
                def _(e, hc=hc, i=i):
                    return e.dma_start(out=dnf[i][:, :],
                                       in_=downT[hc * 128:(hc + 1) * 128, :])
                dnl_tok[i] = p.full(f"d_dn{i}")

            up_load(0)
            up_load(1)
            upb_free = [half_pe, half_pe]
            h1_free = half_pe
            for g8 in range(8):
                up_toks = []
                for k8 in range(8):
                    hc = g8 * 8 + k8
                    i = hc % 2
                    upb = upb2[hc % 2]
                    @p.step("scalar", waits=[upl_tok[i], upb_free[hc % 2]], inc="ac4")
                    def _(e, i=i, upb=upb):
                        return e.sign(out=upb[:, :], in_=upf[i][:, :])
                    sgn_tok = ("ac4", p.cnt["ac4"])
                    upf_free[i] = sgn_tok
                    if hc + 2 < 64:
                        up_load(hc + 2)

                    mm_toks = []
                    for st2 in range(2):
                        bank = pu[st2]
                        @p.step("tensor", waits=[sgn_tok, x1b_tok,
                                                 tokp.get(("puf", st2))], inc="pe4")
                        def _(e, st2=st2, bank=bank, upb=upb):
                            inst = None
                            for cc in range(NCC):
                                inst = e.matmul(
                                    bank[:, :], upb[:, cc * 128:(cc + 1) * 128],
                                    x1b[:, cc * TH + st2 * 512:cc * TH + (st2 + 1) * 512],
                                    start=(cc == 0), stop=(cc == NCC - 1))
                            return inst
                        umm = ("pe4", p.cnt["pe4"])
                        mm_toks.append(umm)

                        @p.step("scalar", waits=[umm] + (h1_free if isinstance(h1_free, list) else [h1_free]), inc="ac4")
                        def _(e, k8=k8, st2=st2, hc=hc, bank=bank):
                            return e.activation(
                                out=h1g[:, k8 * TH + st2 * 512:k8 * TH + (st2 + 1) * 512],
                                in_=bank[:, :], func=AF.Relu,
                                scale=upsc[:, hc:hc + 1])
                        tokp[("puf", st2)] = ("ac4", p.cnt["ac4"])
                    upb_free[hc % 2] = mm_toks[-1]
                h1_tok = ("ac4", p.cnt["ac4"])

                dn_load(g8 * 8)
                dn_load(g8 * 8 + 1)
                dsg_toks = []
                for k8 in range(8):
                    hc = g8 * 8 + k8
                    i = hc % 2
                    @p.step("scalar", waits=[dnl_tok[i], dnb_free if k8 == 0 else None],
                            inc="ac4")
                    def _(e, i=i, k8=k8):
                        return e.sign(out=dnb8[:, k8 * HID:(k8 + 1) * HID],
                                      in_=dnf[i][:, :])
                    dsg_toks.append(("ac4", p.cnt["ac4"]))
                    dnf_free[i] = dsg_toks[-1]
                    if k8 + 2 < 8:
                        dn_load(g8 * 8 + k8 + 2)
                dsg_tok = dsg_toks[-1]

                if g["dbg3_d"] is not None and hf == 0 and g8 == 0:
                    @p.step("sync", waits=[h1_tok], inc="st", k=16)
                    def _(e):
                        return e.dma_start(
                            out=g["dbg3_d"].rearrange("(k p) t -> p k t", p=128),
                            in_=h1g[:, :].rearrange("p (k t) -> p k t", t=1024)
                                [:, :, 0:512])

                for cc in range(NCC):
                    for st2 in range(2):
                        bank = pd[st2]
                        @p.step("tensor", waits=[dsg_tok, h1_tok,
                                                 tokp.get(("pdf", st2))], inc="pe4")
                        def _(e, cc=cc, st2=st2, bank=bank):
                            inst = None
                            for k8 in range(8):
                                inst = e.matmul(
                                    bank[:, :],
                                    dnb8[:, k8 * HID + cc * 128:k8 * HID + (cc + 1) * 128],
                                    h1g[:, k8 * TH + st2 * 512:k8 * TH + (st2 + 1) * 512],
                                    start=(k8 == 0), stop=(k8 == 7))
                            return inst
                        dmm = ("pe4", p.cnt["pe4"])

                        @p.step("vector", waits=[dmm, y2z_tok], inc="dv4")
                        def _(e, cc=cc, st2=st2, bank=bank):
                            return e.tensor_add(
                                out=y2[:, cc * TH + st2 * 512:cc * TH + (st2 + 1) * 512],
                                in0=y2[:, cc * TH + st2 * 512:cc * TH + (st2 + 1) * 512],
                                in1=bank[:, :])
                        tokp[("pdf", st2)] = ("dv4", p.cnt["dv4"])
                dnb_free = tokp[("pdf", 1)]
                h1_free = tokp[("pdf", 1)]
                if g["dbg3_d"] is not None and hf == 0 and g8 == 0:
                    h1_free = [tokp[("pdf", 1)], p.full("st")]
            y2_tok = ("dv4", p.cnt["dv4"])

            xr_free = [None, None]
            for st2 in range(2):
                ts_glob = hf * 2 + st2
                prev_stat = None
                for cc in range(NCC):
                    i = cc % 2
                    @p.step("scalar", waits=[xr_free[i]], inc=f"d_xs{i}", k=16)
                    def _(e, cc=cc, th0=th0, st2=st2, i=i):
                        return e.dma_start(
                            out=xr[i][:, :],
                            in_=x1f_d[cc * 128:(cc + 1) * 128,
                                      th0 + st2 * 512:th0 + (st2 + 1) * 512])
                    xld = p.full(f"d_xs{i}")

                    @p.step("vector", waits=[xld, y2_tok, dse_tok], inc="dv4")
                    def _(e, cc=cc, st2=st2, i=i):
                        return e.scalar_tensor_tensor(
                            out=y2[:, cc * TH + st2 * 512:cc * TH + (st2 + 1) * 512],
                            in0=y2[:, cc * TH + st2 * 512:cc * TH + (st2 + 1) * 512],
                            scalar=dseff[:, cc:cc + 1], in1=xr[i][:, :],
                            op0=ALU.mult, op1=ALU.add)
                    radd = ("dv4", p.cnt["dv4"])
                    xr_free[i] = radd

                    if g["dbg2_d"] is not None and hf == 0 and st2 == 0:
                        @p.step("sync", waits=[radd], inc="st", k=16)
                        def _(e, cc=cc, st2=st2):
                            return e.dma_start(
                                out=g["dbg2_d"][cc * 128:(cc + 1) * 128, :],
                                in_=y2[:, cc * TH + st2 * 512:cc * TH + (st2 + 1) * 512])

                    @p.step("vector", waits=[prev_stat], inc="dv4")
                    def _(e, cc=cc, st2=st2):
                        return e.tensor_copy(
                            out=rb4[:, :],
                            in_=y2[:, cc * TH + st2 * 512:cc * TH + (st2 + 1) * 512])

                    @p.step("vector", inc="dv4")
                    def _(e):
                        return e.tensor_mul(out=rsq4[:, :], in0=rb4[:, :], in1=rb4[:, :])
                    rsq_tok = ("dv4", p.cnt["dv4"])

                    @p.step("tensor", waits=[rsq_tok], inc="pe4")
                    def _(e, cc=cc):
                        e.matmul(ps1[:, :], onescb[:, :], rb4[:, :],
                                 start=(cc == 0), stop=(cc == NCC - 1))
                        return e.matmul(ps2[:, :], onescb[:, :], rsq4[:, :],
                                        start=(cc == 0), stop=(cc == NCC - 1))
                    prev_stat = ("pe4", p.cnt["pe4"])
                stat_mm = ("pe4", p.cnt["pe4"])

                @p.step("vector", waits=[stat_mm], inc="dv4")
                def _(e):
                    return e.tensor_scalar(out=mrow4[:, :], in0=ps1[0:1, :],
                                           scalar1=1.0 / HID, scalar2=None,
                                           op0=ALU.mult)

                @p.step("vector", inc="dv4")
                def _(e):
                    return e.tensor_scalar(out=vrow4[:, :], in0=ps2[0:1, :],
                                           scalar1=1.0 / HID, scalar2=None,
                                           op0=ALU.mult)

                @p.step("vector", inc="dv4")
                def _(e):
                    return e.tensor_mul(out=rb4[0:1, 0:512], in0=mrow4[:, :],
                                        in1=mrow4[:, :])

                @p.step("vector", inc="dv4")
                def _(e):
                    return e.tensor_sub(out=vrow4[:, :], in0=vrow4[:, :],
                                        in1=rb4[0:1, 0:512])

                @p.step("vector", inc="dv4")
                def _(e):
                    return e.tensor_scalar(out=vrow4[:, :], in0=vrow4[:, :],
                                           scalar1=LN_EPS, scalar2=None, op0=ALU.add)
                var_tok = ("dv4", p.cnt["dv4"])

                @p.step("scalar", waits=[var_tok], inc="ac4")
                def _(e):
                    return e.activation(out=vrow4[:, :], in_=vrow4[:, :], func=AF.Ln)

                @p.step("scalar", inc="ac4")
                def _(e):
                    return e.activation(out=vrow4[:, :], in_=vrow4[:, :], func=AF.Exp,
                                        scale=-0.5)
                rstd_tok = ("ac4", p.cnt["ac4"])

                @p.step("tensor", waits=[rstd_tok, tokp.get("prp4")], inc="pe4")
                def _(e):
                    e.matmul(pr1[:, :], onesf[:, :], mrow4[:, :], start=True, stop=True)
                    return e.matmul(pr2[:, :], onesf[:, :], vrow4[:, :],
                                    start=True, stop=True)
                rep_tok = ("pe4", p.cnt["pe4"])

                for cc in range(NCC):
                    sl4 = lambda cc=cc, st2=st2: y2[:, cc * TH + st2 * 512:
                                                    cc * TH + (st2 + 1) * 512]
                    @p.step("vector", waits=[rep_tok, p.full("st")], inc="dv4")
                    def _(e, cc=cc, st2=st2):
                        s = y2[:, cc * TH + st2 * 512:cc * TH + (st2 + 1) * 512]
                        return e.tensor_sub(out=s, in0=s, in1=pr1[:, :])

                    @p.step("vector", inc="dv4")
                    def _(e, cc=cc, st2=st2):
                        s = y2[:, cc * TH + st2 * 512:cc * TH + (st2 + 1) * 512]
                        return e.tensor_mul(out=s, in0=s, in1=pr2[:, :])
                    odone = ("dv4", p.cnt["dv4"])

                    @p.step("sync", waits=[odone], inc="st", k=16)
                    def _(e, cc=cc, st2=st2, ts_glob=ts_glob):
                        return e.dma_start(
                            out=outT_d[cc * 128:(cc + 1) * 128,
                                       ts_glob * 512:(ts_glob + 1) * 512],
                            in_=y2[:, cc * TH + st2 * 512:cc * TH + (st2 + 1) * 512])
                tokp["prp4"] = ("dv4", p.cnt["dv4"])
            tokp["x1b_free"] = ("pe4", p.cnt["pe4"])
            tokp["y2_free"] = p.full("st")


def host_inputs(inputs, b):
    f32 = np.float32
    x = np.ascontiguousarray(np.asarray(inputs["L0_attn_x"][b], f32).T)
    im = {
        "xT": x,
        "wqkvT": _T(inputs["L0_attn_W_qkv"]),
        "wzT": _T(inputs["L0_attn_W_z"]),
        "wbaT": _T(np.concatenate([np.asarray(inputs["L0_attn_W_b"], f32),
                                   np.asarray(inputs["L0_attn_W_a"], f32)], axis=0)),
        "woutT": _T(inputs["L0_attn_W_out"]),
        "upT": _T(inputs["L0_ffn_up_weight"]),
        "downT": _T(inputs["L0_ffn_down_weight"]),
        "convw": np.ascontiguousarray(np.asarray(inputs["L0_attn_conv_w"], f32)),
        "dtb_c": np.asarray(inputs["L0_attn_dt_bias"], f32).reshape(32, 1),
        "alog_c": np.asarray(inputs["L0_attn_A_log"], f32).reshape(32, 1),
        "normw_c": np.asarray(inputs["L0_attn_norm_w"], f32).reshape(128, 1),
        "upsc_c": np.ascontiguousarray(
            np.asarray(inputs["L0_ffn_up_scale"], f32).reshape(64, 128).T),
        "dssc_c": np.ascontiguousarray(
            np.asarray(inputs["L0_ffn_down_scale"], f32).reshape(16, 128).T),
        "osc_c": np.asarray(inputs["L0_ffn_output_scale"], f32).reshape(1, 1),
        "identb_d": np.eye(128, dtype=BF),
        "identf_d": np.eye(128, dtype=np.float32),
        "maskL_d": np.tril(-np.ones((128, 128), BF), -1),
        "maskUs_d": np.triu(-np.ones((128, 128), BF), 1),
        "maskUi_d": np.triu(np.ones((128, 128), BF), 0),
        "Lz_d": np.concatenate([np.zeros((3, 128), BF), np.ones((3, 128), BF)]),
    }
    return im


def _T(w):
    return np.ascontiguousarray(np.asarray(w, np.float32).T)


_NC_CACHE = {}


def get_nc(phases=(1, 2, 25, 34), dump=False):
    key = (tuple(phases), dump)
    if key not in _NC_CACHE:
        _NC_CACHE[key] = build(phases=phases, dump=dump)
    return _NC_CACHE[key]


def kernel(**inputs):
    nc = get_nc()
    in_maps = [host_inputs(inputs, b) for b in range(8)]
    res = run_bass_kernel_spmd(nc, in_maps, core_ids=list(range(8)))
    out = np.stack([np.ascontiguousarray(r["outT"].T) for r in res.results])
    return out.astype(np.float32)
